# version 08
# baseline (speedup 1.0000x reference)
import numpy as np
import ml_dtypes
from contextlib import ExitStack

import concourse.bass as bass
import concourse.mybir as mybir
from concourse.bass_utils import run_bass_kernel_spmd

F32 = mybir.dt.float32
BF16 = mybir.dt.bfloat16
AF = mybir.ActivationFunctionType
ALU = mybir.AluOpType
AX = mybir.AxisListType

TSEQ, DM, NST, NEXP, FF = 4096, 1024, 16, 8, 4096
TCF, NCF = 512, 8
TCM, NCM = 256, 16
STR = 260
NQ = 4
ENGS = ("sync", "gpsimd", "vector", "scalar", "tensor")


def lag(c):
    if c == 0:
        return 0, TCM - 2
    return TCM * c - 2, (TCM + 2 if c == NCM - 1 else TCM)


class Ctr:
    def __init__(self, name):
        self.name = name
        self.v = 0
        self.sem = None


def ap(t, off, dims):
    if isinstance(t, bass.AP):
        t = t.tensor
    return bass.AP(t, off, [list(d) for d in dims])


class KB:
    def __init__(self, nc):
        self.nc = nc
        self.prog = {e: [] for e in ENGS}
        self.pend = []
        self.last_wait = {e: {} for e in ENGS}
        self.n_instr = 0

    def emit(self, eng, fn, waits=(), inc=None, inc_n=1):
        todo = []
        for w in waits:
            if w is None or w == 0:
                continue
            c, v = w
            if v is None or v <= 0:
                continue
            assert v <= c.v, f"wait on future value of {c.name}: {v} > {c.v}"
            lw = self.last_wait[eng].get(c.name, 0)
            if v > lw:
                todo.append((c, v))
                self.last_wait[eng][c.name] = v
        mark = None
        if inc is not None:
            inc.v += inc_n
            mark = (inc, inc.v)
        self.n_instr += 1 + len(todo)

        def replay(eo, todo=todo, fn=fn, inc=inc, n=inc_n):
            for (c, v) in todo:
                eo.wait_ge(c.sem, v)
            ins = fn(eo)
            if inc is not None:
                ins.then_inc(inc.sem, n)
        self.prog[eng].append(replay)
        return mark


def build_core(debug=False, max_rounds=NCM + 2, no_mamba=False):
    nc = bass.Bass("TRN2", target_bir_lowering=False)
    kb = KB(nc)
    es = ExitStack()

    def din(name, shape, dt=F32):
        return nc.dram_tensor(name, list(shape), dt, kind="ExternalInput")

    t_x = din("x", [TSEQ, DM])
    t_win = din("w_in", [DM, 2 * DM])
    t_wout = din("w_out", [2 * DM, DM])
    t_w1e = din("w1e", [NEXP, DM, FF])
    t_w3e = din("w3e", [NEXP, DM, FF])
    t_w2e = din("w2e", [NEXP, FF, DM])
    t_w1s = din("w1s", [DM, FF])
    t_w3s = din("w3s", [DM, FF])
    t_w2s = din("w2s", [FF, DM])
    t_wxp = din("wxp", [DM, 128])
    t_wdt = din("wdt", [64, DM])
    t_gwt = din("gwt", [DM, 8])
    t_aprep = din("aprep", [128, 128])
    t_dcol = din("dcol", [128, 8])
    t_bdtcol = din("bdtcol", [128, 8])
    t_boutcol = din("boutcol", [128, 8])
    t_kxcol = din("kxcol", [128, 32])
    t_kzcol = din("kzcol", [128, 32])
    t_idn = din("idn", [128, 128])
    t_cvec = din("cvec", [128, 2])

    t_out = nc.dram_tensor("out", [TSEQ, DM], F32, kind="ExternalOutput")

    def dint(name, numel, dt=BF16):
        return nc.dram_tensor(name, [numel], dt, kind="Internal")

    cb_win = dint("cb_win", 16 * 128 * 8 * 128)
    cb_wout = dint("cb_wout", 8 * 128 * 16 * 128)
    cb_w1 = dint("cb_w1", NEXP * 32 * 128 * 8 * 128)
    cb_w3 = dint("cb_w3", NEXP * 32 * 128 * 8 * 128)
    cb_w2 = dint("cb_w2", NEXP * 8 * 128 * 32 * 128)
    cb_w1s = dint("cb_w1s", 32 * 128 * 8 * 128)
    cb_w3s = dint("cb_w3s", 32 * 128 * 8 * 128)
    cb_w2s = dint("cb_w2s", 8 * 128 * 32 * 128)
    bdram = nc.dram_tensor("bdram", [NST, TSEQ], F32, kind="Internal")
    cdram = nc.dram_tensor("cdram", [NST, TSEQ], F32, kind="Internal")
    gdram = nc.dram_tensor("gdram", [8, TSEQ], F32, kind="Internal")
    fdram = nc.dram_tensor("fdram", [TSEQ, DM], F32, kind="Internal")
    pdram = nc.dram_tensor("pdram", [TSEQ, DM], F32, kind="Internal")

    dbg = {}
    if debug:
        dbg["d_xs"] = nc.dram_tensor("d_xs", [NCM, 128, 8 * STR], F32,
                                     kind="ExternalOutput")
        dbg["d_delta"] = nc.dram_tensor("d_delta", [NCM, 128, 8 * STR], F32,
                                        kind="ExternalOutput")
        dbg["d_y"] = nc.dram_tensor("d_y", [NCM, 128, 8 * STR], F32,
                                    kind="ExternalOutput")
        dbg["d_acc"] = nc.dram_tensor("d_acc", [NCF, 128, 8 * TCF], F32,
                                      kind="ExternalOutput")
        dbg["d_gates"] = nc.dram_tensor("d_gates", [8, TSEQ], F32,
                                        kind="ExternalOutput")
        dbg["d_xn"] = nc.dram_tensor("d_xn", [NCF, 128, 8 * TCF], BF16,
                                     kind="ExternalOutput")
        dbg["d_hid"] = nc.dram_tensor("d_hid", [128, 32 * TCF], BF16,
                                      kind="ExternalOutput")
        dbg["d_cbw1"] = nc.dram_tensor("d_cbw1", [32 * 128 * 8 * 128], BF16,
                                       kind="ExternalOutput")
        dbg["d_xn2"] = nc.dram_tensor("d_xn2", [128, 8 * TCF], BF16,
                                      kind="ExternalOutput")
        dbg["d_gsb"] = nc.dram_tensor("d_gsb", [128, TCF], F32,
                                      kind="ExternalOutput")
        dbg["d_wr1"] = nc.dram_tensor("d_wr1", [128, 1024], BF16,
                                      kind="ExternalOutput")

    def sb(name, cols, dt=F32, parts=128):
        return es.enter_context(nc.sbuf_tensor(name, [parts, cols], dt))

    c_aneg = sb("c_aneg", 128)
    c_d = sb("c_d", 8)
    c_bdt = sb("c_bdt", 8)
    c_bout = sb("c_bout", 8)
    c_kx = sb("c_kx", 32)
    c_kz = sb("c_kz", 32)
    c_idn = sb("c_idn", 128)
    c_wxp = sb("c_wxp", 8 * 128, BF16)
    c_wdt = sb("c_wdt", DM, BF16, parts=64)
    c_ghi = sb("c_ghi", 64, BF16)
    c_glo = sb("c_glo", 64, BF16)
    c_cv = sb("c_cv", 2)
    c_mneg = sb("c_mneg", 32)
    c_rs = sb("c_rs", 32)

    m_ext = sb("m_ext", 264)
    m_cpre = sb("m_cpre", 2 * STR)
    m_carry = sb("m_carry", 48)
    m_xsconv = sb("m_xsconv", 8 * STR)
    m_xsbf = sb("m_xsbf", 8 * STR, BF16)
    m_zbf = sb("m_zbf", 8 * STR, BF16)
    m_dtlow = sb("m_dtlow", STR, BF16, parts=64)
    m_bc = sb("m_bc", STR)
    m_delta = sb("m_delta", 8 * STR)
    m_dtu = sb("m_dtu", 8 * STR)
    m_dA = sb("m_dA", 2 * NQ * STR)
    m_dBu = sb("m_dBu", NQ * STR)
    m_h = m_dBu
    m_brep = sb("m_brep", NQ * STR)
    m_crep = sb("m_crep", NQ * STR)
    m_ytmp = sb("m_ytmp", STR)
    m_y = sb("m_y", 8 * STR)
    m_ybf = sb("m_ybf", 8 * STR, BF16)
    m_hst = sb("m_hst", 128)
    m_ast = sb("m_ast", 8 * STR)
    m_ptm = sb("m_ptm", DM)
    m_xlag = sb("m_xlag", DM)

    f_x = sb("f_x", 2 * DM)
    f_xntm = sb("f_xntm", DM)
    f_xn = sb("f_xn", 8 * TCF, BF16)
    f_hid = sb("f_hid", 32 * TCF, BF16)
    f_acc = sb("f_acc", 8 * TCF)
    f_gb = sb("f_gb", 2 * TCF)
    f_gsb = sb("f_gsb", 3 * TCF)
    f_tm = sb("f_tm", DM)
    w_r1 = sb("w_r1", 4 * 1024, BF16)
    w_r2 = sb("w_r2", 4 * 2048, BF16)
    w_rin = sb("w_rin", 2 * 1024, BF16)
    w_rout = sb("w_rout", 2 * 1024, BF16)
    g_pad = m_ext
    g_gt = m_cpre
    g_sm = m_dA
    f_wt = ap(m_ast, 0, [[8 * STR, 128], [1, TCF]])

    M = [es.enter_context(nc.psum_tensor(f"pm{i}", [128, 512], F32))
         for i in range(8)]

    sV, sS, sT, sGp = Ctr("sV"), Ctr("sS"), Ctr("sT"), Ctr("sGp")
    stream_defs = [
        ("cst", "sync"),
        ("win0", "scalar"), ("win1", "scalar"),
        ("wo0", "scalar"), ("wo1", "scalar"),
        ("w1a", "sync"), ("w1b", "sync"), ("w1c", "sync"), ("w1d", "sync"),
        ("w2a", "gpsimd"), ("w2b", "gpsimd"), ("w2c", "gpsimd"), ("w2d", "gpsimd"),
        ("fx0", "scalar"), ("fx1", "scalar"),
        ("bcw", "scalar"), ("gdw", "scalar"),
        ("brb", "scalar"), ("crb", "scalar"),
        ("gb0", "scalar"), ("gb1", "scalar"),
        ("ptw", "scalar"), ("ftw0", "scalar"), ("ftw1", "scalar"),
        ("outw", "scalar"), ("xlg", "scalar"), ("mpt", "scalar"),
        ("dbgs", "scalar"),
        ("cwin", "gpsimd"), ("cwout", "gpsimd"),
    ] + [(f"ce{e}", "gpsimd") for e in range(9)]
    SQ = {}
    for name, q in stream_defs:
        SQ[name] = (q, Ctr(name))
    for c in (sV, sS, sT, sGp):
        c.sem = es.enter_context(nc.semaphore(c.name))
    for name, (q, c) in SQ.items():
        c.sem = es.enter_context(nc.semaphore(name))

    def dma(stream, out, in_, waits=(), **kw):
        q, c = SQ[stream]
        return kb.emit(q, lambda e: e.dma_start(out=out, in_=in_, **kw),
                       waits, inc=c, inc_n=16)

    bankfree = {i: None for i in range(8)}

    def bwait(i):
        return [bankfree[i]]

    castmark = {}

    def cast_cb(stream, dst, src, D_, F_, off_src, off_dst, ksplit=8):
        nk, nm = D_ // 128, F_ // 128
        mk = None
        for m in range(nm):
            for k0 in range(0, nk, ksplit):
                ks = min(ksplit, nk - k0)
                mk = dma(
                    stream,
                    ap(dst, off_dst + m * (128 * nk * 128) + k0 * 128,
                       [[nk * 128, 128], [128, ks], [1, 128]]),
                    ap(src, off_src + (k0 * 128) * F_ + m * 128,
                       [[F_, 128], [128 * F_, ks], [1, 128]]))
        return mk

    castmark["win"] = cast_cb("cwin", cb_win, t_win.ap(), DM, 2 * DM, 0, 0)
    castmark["wout"] = cast_cb("cwout", cb_wout, t_wout.ap(), 2 * DM, DM, 0, 0)
    for e in range(NEXP):
        st = f"ce{e}"
        cast_cb(st, cb_w1, t_w1e.ap(), DM, FF, e * DM * FF,
                e * 32 * 128 * 8 * 128)
        cast_cb(st, cb_w3, t_w3e.ap(), DM, FF, e * DM * FF,
                e * 32 * 128 * 8 * 128)
        castmark[("e", e)] = cast_cb(st, cb_w2, t_w2e.ap(), FF, DM,
                                     e * FF * DM, e * 8 * 128 * 32 * 128)
    cast_cb("ce8", cb_w1s, t_w1s.ap(), DM, FF, 0, 0)
    cast_cb("ce8", cb_w3s, t_w3s.ap(), DM, FF, 0, 0)
    castmark[("e", 8)] = cast_cb("ce8", cb_w2s, t_w2s.ap(), FF, DM, 0, 0)

    mk_ap = dma("cst", c_aneg[:], t_aprep.ap())
    dma("cst", c_d[:], t_dcol.ap())
    dma("cst", c_bdt[:], t_bdtcol.ap())
    dma("cst", c_bout[:], t_boutcol.ap())
    dma("cst", c_kx[:], t_kxcol.ap())
    dma("cst", c_kz[:], t_kzcol.ap())
    mk_id = dma("cst", c_idn[:], t_idn.ap())
    dma("cst", c_cv[:], t_cvec.ap())
    for k in range(8):
        mw = dma("cst", f_xntm[:, 0:128],
                 ap(t_wxp.ap(), k * 128 * 128, [[128, 128], [1, 128]]),
                 waits=[(sV, sV.v)])
        kb.emit("vector",
                lambda e, k=k: e.tensor_copy(c_wxp[:, k * 128:(k + 1) * 128],
                                             f_xntm[:, 0:128]),
                waits=[mw], inc=sV)
    mw = dma("cst", ap(m_ast, 0, [[8 * STR, 64], [1, DM]]),
             ap(t_wdt.ap(), 0, [[DM, 64], [1, DM]]))
    kb.emit("vector",
            lambda e: e.tensor_copy(c_wdt[0:64, :],
                                    ap(m_ast, 0, [[8 * STR, 64], [1, DM]])),
            waits=[mw], inc=sV)
    for k in range(8):
        mw = dma("cst", g_pad[:, 0:8],
                 ap(t_gwt.ap(), k * 128 * 8, [[8, 128], [1, 8]]),
                 waits=[(sV, sV.v)])
        mgh = kb.emit("vector",
                      lambda e, k=k: e.tensor_copy(c_ghi[:, k * 8:(k + 1) * 8],
                                                   g_pad[:, 0:8]),
                      waits=[mw], inc=sV)
        kb.emit("vector",
                lambda e, k=k: e.tensor_tensor(
                    c_glo[:, k * 8:(k + 1) * 8], g_pad[:, 0:8],
                    c_ghi[:, k * 8:(k + 1) * 8], ALU.subtract),
                waits=[mgh], inc=sV)
    mk_cv0 = (SQ["cst"][1], SQ["cst"][1].v)
    ms = kb.emit("scalar", lambda e: e.activation(c_aneg[:], c_aneg[:], AF.Exp),
                 waits=[mk_ap, mk_cv0], inc=sS)
    kb.emit("vector", lambda e: e.tensor_scalar_mul(c_aneg[:], c_aneg[:], -1.0),
            waits=[ms], inc=sV)
    kb.emit("vector", lambda e: e.memset(m_carry[:], 0.0), inc=sV)
    kb.emit("vector", lambda e: e.memset(m_hst[:], 0.0), inc=sV)
    kb.emit("vector", lambda e: e.memset(g_pad[:], 0.0), inc=sV)
    kb.emit("vector", lambda e: e.memset(g_gt[:], 0.0), inc=sV)
    consts_v = (sV, sV.v)
    mk_cv = (SQ["cst"][1], SQ["cst"][1].v)

    norm_done = {}
    tile_trans_done = None
    for i in range(32):
        slot = i % 2
        mx = dma(f"fx{slot}", f_x[:, slot * DM:(slot + 1) * DM],
                 ap(t_x.ap(), i * 128 * DM, [[DM, 128], [1, DM]]),
                 waits=[norm_done.get(i - 2)])
        mrd = kb.emit("vector",
                      lambda e, s=slot: e.reduce_sum(
                          m_ytmp[:, 0:1], f_x[:, s * DM:(s + 1) * DM], AX.X),
                      waits=[mx, consts_v, mk_cv], inc=sV)
        mv = kb.emit("vector",
                     lambda e, i=i: e.tensor_scalar_mul(
                         c_mneg[:, i:i + 1], m_ytmp[:, 0:1], -1.0 / DM),
                     waits=[mrd], inc=sV)
        ms = kb.emit("scalar",
                     lambda e, s=slot, i=i: e.activation(
                         f_acc[:, 0:DM], f_x[:, s * DM:(s + 1) * DM],
                         AF.Square, bias=c_mneg[:, i:i + 1],
                         accum_out=c_rs[:, i:i + 1]),
                     waits=[mv], inc=sS)
        ms = kb.emit("scalar",
                     lambda e, i=i: e.activation(c_rs[:, i:i + 1],
                                                 c_rs[:, i:i + 1],
                                                 AF.Ln, bias=c_cv[:, 0:1],
                                                 scale=1.0 / DM),
                     waits=[ms], inc=sS)
        ms = kb.emit("scalar",
                     lambda e, i=i: e.activation(
                         c_rs[:, i:i + 1], c_rs[:, i:i + 1], AF.Exp,
                         scale=-0.5),
                     waits=[ms], inc=sS)
        mv = kb.emit("vector",
                     lambda e, s=slot, i=i: e.tensor_scalar(
                         f_xntm[:], f_x[:, s * DM:(s + 1) * DM],
                         c_mneg[:, i:i + 1], c_rs[:, i:i + 1],
                         ALU.add, ALU.mult),
                     waits=[ms, tile_trans_done], inc=sV)
        norm_done[i] = mv
        mhi = kb.emit("vector",
                      lambda e: e.tensor_copy(m_xsbf[:, 0:DM], f_xntm[:]),
                      waits=[mv], inc=sV)
        mlo = kb.emit("vector",
                      lambda e: e.tensor_tensor(m_xsconv[:, 0:DM], f_xntm[:],
                                                m_xsbf[:, 0:DM], ALU.subtract),
                      waits=[mhi], inc=sV)
        last_ev = None
        for which, src in ((0, f_xntm), (1, m_xsconv)):
            for half in range(2):
                bank = 6 + half
                grp = None
                for q in range(4):
                    d = half * 4 + q
                    grp = kb.emit(
                        "tensor",
                        lambda e, b=bank, q=q, src=src, d=d: e.transpose(
                            M[b][:, q * 128:(q + 1) * 128],
                            src[:, d * 128:d * 128 + 128], c_idn[:]),
                        waits=([mlo, mk_id] + bwait(bank)) if q == 0 else [],
                        inc=(sT if q == 3 else None))
                dst0 = which * 1024 + half * 512
                mse = kb.emit(
                    "scalar",
                    lambda e, b=bank, d0=dst0: e.copy(
                        f_xn[:, d0:d0 + 512], M[b][:, 0:512]),
                    waits=[grp], inc=sS)
                bankfree[bank] = mse
                last_ev = mse
        nmm = 0
        grp = None
        for (lh, rh0) in ((c_ghi, 0), (c_glo, 0), (c_ghi, 1024)):
            for k in range(8):
                st0, sp0 = (nmm == 0), (nmm == 23)
                grp = kb.emit(
                    "tensor",
                    lambda e, lh=lh, k=k, rh0=rh0, st0=st0, sp0=sp0: e.matmul(
                        M[4][0:8, 0:128], lh[:, k * 8:(k + 1) * 8],
                        f_xn[:, rh0 + k * 128:rh0 + k * 128 + 128],
                        start=st0, stop=sp0),
                    waits=([last_ev, consts_v] + bwait(4))
                    if nmm == 0 else [],
                    inc=(sT if sp0 else None))
                nmm += 1
        mv = kb.emit("vector",
                     lambda e: e.tensor_copy(g_pad[0:8, 0:128],
                                             M[4][0:8, 0:128]),
                     waits=[grp], inc=sV)
        bankfree[4] = mv
        mt = kb.emit("tensor",
                     lambda e: e.transpose(M[5][:, 0:128], g_pad[:, 0:128],
                                           c_idn[:]),
                     waits=[mv] + bwait(5), inc=sT)
        mv = kb.emit("vector",
                     lambda e: e.tensor_copy(g_sm[:, 0:8], M[5][:, 0:8]),
                     waits=[mt], inc=sV)
        bankfree[5] = mv
        kb.emit("vector",
                lambda e: e.reduce_max(g_sm[:, 40:41], g_sm[:, 0:8], AX.X,
                                       negate=True), waits=[mv], inc=sV)
        ms = kb.emit("scalar",
                     lambda e: e.activation(g_sm[:, 8:16], g_sm[:, 0:8],
                                            AF.Exp, bias=g_sm[:, 40:41]),
                     waits=[(sV, sV.v)], inc=sS)
        mq = kb.emit("vector",
                     lambda e: e.reduce_sum(g_sm[:, 41:42], g_sm[:, 8:16],
                                            AX.X),
                     waits=[ms], inc=sV)
        mq = kb.emit("vector",
                     lambda e: e.reciprocal(g_sm[:, 42:43], g_sm[:, 41:42]),
                     waits=[mq], inc=sV)
        mq = kb.emit("vector",
                     lambda e: e.tensor_scalar_mul(g_sm[:, 0:8], g_sm[:, 8:16],
                                                   g_sm[:, 42:43]),
                     waits=[mq], inc=sV)
        mq = kb.emit("vector",
                     lambda e: e.tensor_single_scalar(g_sm[:, 8:16],
                                                      g_sm[:, 0:8],
                                                      g_sm[:, 42:43],
                                                      ALU.is_ge),
                     waits=[mq], inc=sV)
        mq = kb.emit("vector",
                     lambda e: e.scalar_tensor_tensor(g_sm[:, 16:24],
                                                      g_sm[:, 8:16],
                                                      -2.0, g_sm[:, 0:8],
                                                      ALU.mult, ALU.add),
                     waits=[mq], inc=sV)
        mq = kb.emit("vector",
                     lambda e: e.reduce_max(g_sm[:, 43:44], g_sm[:, 16:24],
                                            AX.X),
                     waits=[mq], inc=sV)
        mq = kb.emit("vector",
                     lambda e: e.tensor_single_scalar(g_sm[:, 8:16],
                                                      g_sm[:, 0:8],
                                                      g_sm[:, 43:44],
                                                      ALU.is_ge),
                     waits=[mq], inc=sV)
        mq = kb.emit("vector",
                     lambda e: e.scalar_tensor_tensor(g_sm[:, 16:24],
                                                      g_sm[:, 0:8],
                                                      0.5, g_sm[:, 8:16],
                                                      ALU.mult, ALU.mult),
                     waits=[mq], inc=sV)
        mv = kb.emit("vector",
                     lambda e: e.tensor_copy(g_gt[:, 0:8], g_sm[:, 16:24]),
                     waits=[mq], inc=sV)
        mt = kb.emit("tensor",
                     lambda e: e.transpose(M[5][:, 0:128], g_gt[:, 0:128],
                                           c_idn[:]),
                     waits=[mv] + bwait(5), inc=sT)
        mv = kb.emit("vector",
                     lambda e: e.tensor_copy(g_pad[0:8, 0:128],
                                             M[5][0:8, 0:128]),
                     waits=[mt], inc=sV)
        bankfree[5] = mv
        dma("gdw", ap(gdram.ap(), i * 128, [[TSEQ, 8], [1, 128]]),
            g_pad[0:8, 0:128], waits=[mv])
        tile_trans_done = (sT, sT.v)
    gdram_mark = (SQ["gdw"][1], SQ["gdw"][1].v)
    if debug:
        dma("dbgs", dbg["d_gates"].ap(), gdram.ap(), waits=[gdram_mark])
        dma("dbgs", dbg["d_cbw1"].ap(),
            ap(cb_w1, 0, [[131072, 32], [512, 256], [1, 512]]),
            waits=[castmark[("e", 0)]])

    rst = {
        "xn_ready": None, "xn_users": [], "xn_tr_last": None,
        "ybf": {}, "attn_mm_done": None, "attn_tr_last": None,
        "partial": {}, "fwrit": {},
        "acc_free": None, "gb_mark": {},
        "gsb_free": {}, "dAslot_free": {},
        "brep_free": None, "gwin": 0, "gw13": 0, "gw2": 0, "gwout": 0,
        "winfree": {}, "w1slot": {}, "w2slot": {}, "woutslot": {},
        "delta_mark": None, "dtu_done": None, "bc_mark": None,
        "delta_mm_last": None, "exp_marks": {}, "xlag_free": None,
        "ptm_free": None, "ybf_all": None,
        "xs_dump": None, "y_dump": None, "acc_dump": None, "xn_dump": None,
        "hid_dump": None, "wr1_dump": None, "fwt_free": None,
    }

    def xn_build(f):
        guard = rst["xn_users"]
        rst["xn_users"] = []
        for i4 in range(4):
            gi = 4 * f + i4
            slot = i4 % 2
            key = ("r", f, i4 - 2) if i4 >= 2 else ("r", f - 1, i4 + 2)
            prevn = norm_done.get(key, norm_done.get(31) if f == 0 else None)
            mx = dma(f"fx{slot}", f_x[:, slot * DM:(slot + 1) * DM],
                     ap(t_x.ap(), gi * 128 * DM, [[DM, 128], [1, DM]]),
                     waits=[prevn])
            mv = kb.emit("vector",
                         lambda e, s=slot, gi=gi: e.tensor_scalar(
                             f_xntm[:], f_x[:, s * DM:(s + 1) * DM],
                             c_mneg[:, gi:gi + 1], c_rs[:, gi:gi + 1],
                             ALU.add, ALU.mult),
                         waits=[mx, rst["xn_tr_last"]] + guard, inc=sV)
            guard = []
            norm_done[("r", f, i4)] = mv
            for half in range(2):
                bank = 6 + half
                grp = None
                for q in range(4):
                    d = half * 4 + q
                    grp = kb.emit(
                        "tensor",
                        lambda e, b=bank, q=q, d=d: e.transpose(
                            M[b][:, q * 128:(q + 1) * 128],
                            f_xntm[:, d * 128:d * 128 + 128], c_idn[:]),
                        waits=([mv] + bwait(bank)) if q == 0 else [],
                        inc=(sT if q == 3 else None))
                rst["xn_tr_last"] = grp
                mse = kb.emit(
                    "scalar",
                    lambda e, b=bank, h=half, i4=i4: e.copy(
                        ap(f_xn, (h * 4) * TCF + i4 * 128,
                           [[8 * TCF, 128], [TCF, 4], [1, 128]]),
                        ap(M[b], 0, [[512, 128], [128, 4], [1, 128]])),
                    waits=[grp, rst["xn_dump"]], inc=sS)
                bankfree[bank] = mse
                rst["xn_ready"] = mse
        if debug:
            rst["xn_dump"] = dma(
                "dbgs",
                ap(dbg["d_xn"].ap(), f * 128 * 8 * TCF,
                   [[8 * TCF, 128], [1, 8 * TCF]]),
                f_xn[:], waits=[rst["xn_ready"]])

    def mamba_xz_conv(c):
        S, W = lag(c)
        off = (c % 2) * TCM
        silu_marks = {}
        for j in range(16):
            g_i = rst["gwin"]
            rst["gwin"] += 1
            slot = g_i % 2
            wmk = dma(
                f"win{slot}", w_rin[:, slot * 1024:(slot + 1) * 1024],
                ap(cb_win, j * 128 * 8 * 128, [[8 * 128, 128], [1, 8 * 128]]),
                waits=[castmark["win"], rst["winfree"].get(g_i - 2)])
            bank = (4, 5, 6)[j % 3]
            grp = None
            for k in range(8):
                grp = kb.emit(
                    "tensor",
                    lambda e, b=bank, s=slot, k=k, off=off: e.matmul(
                        M[b][:, 0:TCM],
                        w_rin[:, s * 1024 + k * 128:s * 1024 + k * 128 + 128],
                        f_xn[:, k * TCF + off:k * TCF + off + TCM],
                        start=(k == 0), stop=(k == 7)),
                    waits=([wmk, rst["xn_ready"]] + bwait(bank))
                    if k == 0 else [],
                    inc=(sT if k == 7 else None))
            rst["winfree"][g_i] = grp
            if j == 15:
                rst["xn_users"].append(grp)
            kb.emit("vector",
                    lambda e, j=j: e.tensor_copy(m_ext[:, 0:3],
                                                 m_carry[:, j * 3:j * 3 + 3]),
                    waits=[silu_marks.get(j - 2), rst["delta_mark"]])
            mv = kb.emit("vector",
                         lambda e, b=bank: e.tensor_copy(m_ext[:, 3:3 + TCM],
                                                         M[b][:, 0:TCM]),
                         waits=[grp], inc=sV)
            bankfree[bank] = mv
            if c == NCM - 1:
                kb.emit("vector",
                        lambda e: e.memset(m_ext[:, 3 + TCM:3 + TCM + 2], 0.0))
            cs = (j % 2) * STR
            kc = c_kx if j < 8 else c_kz
            g = j % 8
            eo = 2 if c == 0 else 0
            mq = kb.emit("vector",
                         lambda e, cs=cs, kc=kc, g=g, eo=eo, W=W:
                         e.tensor_scalar_mul(m_cpre[:, cs:cs + W],
                                             m_ext[:, eo:eo + W],
                                             kc[:, g * 4:g * 4 + 1]),
                         waits=[mv], inc=sV)
            for w in range(1, 4):
                mq = kb.emit("vector",
                             lambda e, cs=cs, kc=kc, g=g, w=w, eo=eo, W=W:
                             e.scalar_tensor_tensor(
                                 m_cpre[:, cs:cs + W],
                                 m_ext[:, eo + w:eo + w + W],
                                 kc[:, g * 4 + w:g * 4 + w + 1],
                                 m_cpre[:, cs:cs + W], ALU.mult, ALU.add),
                             waits=[mq], inc=sV)
            mv = kb.emit("vector",
                         lambda e, j=j: e.tensor_copy(
                             m_carry[:, j * 3:j * 3 + 3],
                             m_ext[:, TCM:TCM + 3]),
                         inc=sV)
            if j < 8:
                kb.emit("scalar",
                        lambda e, cs=cs, g=g, W=W: e.activation(
                            m_xsconv[:, g * STR:g * STR + W],
                            m_cpre[:, cs:cs + W], AF.Silu),
                        waits=[mv, rst["attn_mm_done"], rst["ybf_all"],
                               rst["xs_dump"]])
                sm = kb.emit("scalar",
                             lambda e, cs=cs, g=g, W=W: e.activation(
                                 m_xsbf[:, g * STR:g * STR + W],
                                 m_cpre[:, cs:cs + W], AF.Silu),
                             inc=sS)
            else:
                sm = kb.emit("scalar",
                             lambda e, cs=cs, g=g, W=W: e.activation(
                                 m_zbf[:, g * STR:g * STR + W],
                                 m_cpre[:, cs:cs + W], AF.Silu),
                             waits=[mv, rst["attn_mm_done"]],
                             inc=sS)
            silu_marks[j] = sm
        rst["xs_ready"] = silu_marks[7]
        rst["z_ready"] = silu_marks[15]
        if debug:
            rst["xs_dump"] = dma(
                "dbgs",
                ap(dbg["d_xs"].ap(), c * 128 * 8 * STR,
                   [[8 * STR, 128], [1, 8 * STR]]),
                m_xsconv[:], waits=[rst["xs_ready"]])

    def mamba_xdbl(c):
        S, W = lag(c)
        grp = None
        for k in range(8):
            grp = kb.emit(
                "tensor",
                lambda e, k=k, W=W: e.matmul(
                    M[6][:, 0:W], c_wxp[:, k * 128:(k + 1) * 128],
                    m_xsbf[:, k * STR:k * STR + W],
                    start=(k == 0), stop=(k == 7)),
                waits=([rst["xs_ready"], consts_v] + bwait(6))
                if k == 0 else [],
                inc=(sT if k == 7 else None))
        mv = kb.emit("vector",
                     lambda e, W=W: e.tensor_copy(
                         ap(m_bc, 64 * STR, [[STR, 48], [1, W]]),
                         ap(M[6], 64 * 512, [[512, 48], [1, W]])),
                     waits=[grp, rst["bc_mark"]], inc=sV)
        ms = kb.emit("scalar",
                     lambda e, W=W: e.copy(m_dtlow[0:64, 0:W],
                                           M[6][0:64, 0:W]),
                     waits=[grp, rst["delta_mm_last"]], inc=sS)
        bankfree[6] = ms
        rst["dtlow_mark"] = ms
        dma("bcw", ap(bdram.ap(), S, [[TSEQ, 16], [1, W]]),
            ap(m_bc, 64 * STR, [[STR, 16], [1, W]]),
            waits=[mv, rst["brep_free"]])
        rst["bc_mark"] = dma("bcw",
                             ap(cdram.ap(), S, [[TSEQ, 16], [1, W]]),
                             ap(m_bc, 96 * STR, [[STR, 16], [1, W]]))

    def mamba_delta(c):
        S, W = lag(c)
        last_ln = None
        for g in range(8):
            bank = 6 + g % 2
            mm = kb.emit(
                "tensor",
                lambda e, b=bank, g=g, W=W: e.matmul(
                    M[b][:, 0:W], c_wdt[0:64, g * 128:(g + 1) * 128],
                    m_dtlow[0:64, 0:W], start=True, stop=True),
                waits=[rst["dtlow_mark"], consts_v] + bwait(bank), inc=sT)
            rst["delta_mm_last"] = mm
            cs = (g % 2) * STR
            mexp = kb.emit("scalar",
                           lambda e, b=bank, cs=cs, g=g, W=W: e.activation(
                               m_cpre[:, cs:cs + W], M[b][:, 0:W], AF.Exp,
                               bias=c_bdt[:, g:g + 1]),
                           waits=[mm, rst["dtu_done"]], inc=sS)
            last_ln = kb.emit("scalar",
                              lambda e, cs=cs, g=g, W=W: e.activation(
                                  m_delta[:, g * STR:g * STR + W],
                                  m_cpre[:, cs:cs + W], AF.Ln,
                                  bias=c_cv[:, 1:2]),
                              waits=[mexp], inc=sS)
            bankfree[bank] = last_ln
        rst["delta_mark"] = last_ln
        rst["exp_marks"] = {}

    def scan_nq(c, nq):
        S, W = lag(c)
        for g in range(8):
            par = (nq * 8 + g) % 2
            base = par * NQ * STR
            mk = None
            for nl in range(NQ):
                n = nq * NQ + nl
                mk = kb.emit(
                    "scalar",
                    lambda e, b=base, nl=nl, g=g, n=n, W=W: e.activation(
                        m_dA[:, b + nl * STR:b + nl * STR + W],
                        m_delta[:, g * STR:g * STR + W], AF.Exp,
                        scale=c_aneg[:, g * 16 + n:g * 16 + n + 1]),
                    waits=[rst["dAslot_free"].get(par), rst["delta_mark"],
                           consts_v]
                    if nl == 0 else [],
                    inc=(sS if nl == NQ - 1 else None))
            rst["exp_marks"][(nq, g)] = mk

        def q_bcast(nq=nq, S=S, W=W):
            dma("brb",
                ap(m_brep, 0, [[NQ * STR, 128], [STR, NQ], [1, W]]),
                ap(bdram.ap(), nq * NQ * TSEQ + S,
                   [[0, 128], [TSEQ, NQ], [1, W]]),
                waits=[rst["bc_mark"], rst["brep_free"]])
            rst["mcc"] = dma("crb",
                             ap(m_crep, 0, [[NQ * STR, 128], [STR, NQ],
                                            [1, W]]),
                             ap(cdram.ap(), nq * NQ * TSEQ + S,
                                [[0, 128], [TSEQ, NQ], [1, W]]))
            rst["mbb"] = (SQ["brb"][1], SQ["brb"][1].v)
        kb.pend.append(q_bcast)
        for g in range(8):
            kb.pend.append(lambda c=c, nq=nq, g=g: scan_unit(c, nq, g))

    def scan_unit(c, nq, g):
        S, W = lag(c)
        mcc, mbb = rst["mcc"], rst["mbb"]
        if True:
            if nq == 0:
                mdt = kb.emit("vector",
                              lambda e, g=g, W=W: e.tensor_tensor(
                                  m_dtu[:, g * STR:g * STR + W],
                                  m_delta[:, g * STR:g * STR + W],
                                  m_xsconv[:, g * STR:g * STR + W], ALU.mult),
                              waits=[rst["delta_mark"]], inc=sV)
                rst["dtu_mark"] = mdt
            par = (nq * 8 + g) % 2
            base = par * NQ * STR
            mq = None
            for nl in range(NQ):
                mq = kb.emit("vector",
                             lambda e, g=g, nl=nl, W=W: e.tensor_tensor(
                                 m_dBu[:, nl * STR:nl * STR + W],
                                 m_dtu[:, g * STR:g * STR + W],
                                 m_brep[:, nl * STR:nl * STR + W], ALU.mult),
                             waits=[mcc, mbb, rst["exp_marks"][(nq, g)],
                                    rst.get("dtu_mark")]
                             if nl == 0 else [],
                             inc=(sV if nl == NQ - 1 else None))
            for nl in range(NQ):
                n = nq * NQ + nl
                mq = kb.emit("vector",
                             lambda e, b=base, nl=nl, g=g, n=n, W=W:
                             e.tensor_tensor_scan(
                                 m_h[:, nl * STR:nl * STR + W],
                                 m_dA[:, b + nl * STR:b + nl * STR + W],
                                 m_dBu[:, nl * STR:nl * STR + W],
                                 m_hst[:, g * 16 + n:g * 16 + n + 1],
                                 ALU.mult, ALU.add),
                             waits=[mq] if nl == 0 else [],
                             inc=(sV if nl == NQ - 1 else None))
            kb.emit("vector",
                    lambda e, g=g, nq=nq, W=W: e.tensor_copy(
                        m_hst[:, g * 16 + nq * NQ:g * 16 + nq * NQ + NQ],
                        ap(m_h, W - 1, [[NQ * STR, 128], [STR, NQ]])),
                    waits=[mq])
            for nl in range(NQ):
                mq = kb.emit("vector",
                             lambda e, b=base, nl=nl, W=W: e.tensor_tensor(
                                 m_dA[:, b + nl * STR:b + nl * STR + W],
                                 m_h[:, nl * STR:nl * STR + W],
                                 m_crep[:, nl * STR:nl * STR + W], ALU.mult),
                             inc=(sV if nl == NQ - 1 else None))
            mq = kb.emit("vector",
                         lambda e, b=base, W=W: e.tensor_reduce(
                             m_ytmp[:, 0:W],
                             ap(m_dA, b,
                                [[2 * NQ * STR, 128], [1, W], [STR, NQ]]),
                             AX.X, ALU.add),
                         waits=[mq], inc=sV)
            if nq == 0:
                mv = kb.emit("vector",
                             lambda e, g=g, W=W: e.tensor_copy(
                                 m_y[:, g * STR:g * STR + W], m_ytmp[:, 0:W]),
                             waits=[mq] + ([rst["y_dump"]] if g == 0 else []),
                             inc=sV)
            else:
                mv = kb.emit("vector",
                             lambda e, g=g, W=W: e.tensor_add(
                                 m_y[:, g * STR:g * STR + W],
                                 m_y[:, g * STR:g * STR + W], m_ytmp[:, 0:W]),
                             waits=[mq], inc=sV)
            rst["dAslot_free"][par] = mv
        rst["brep_free"] = (sV, sV.v)
        if nq == 0:
            rst["dtu_done"] = (sV, sV.v)

    def scan_finish(c):
        S, W = lag(c)
        mv = None
        for g in range(8):
            mv = kb.emit("vector",
                         lambda e, g=g, W=W: e.scalar_tensor_tensor(
                             m_ybf[:, g * STR:g * STR + W],
                             m_xsconv[:, g * STR:g * STR + W],
                             c_d[:, g:g + 1], m_y[:, g * STR:g * STR + W],
                             ALU.mult, ALU.add),
                         waits=[rst["attn_mm_done"]] if g == 0 else [],
                         inc=(sV if g == 7 else None))
        rst["ybf"][c] = mv
        rst["ybf_all"] = mv
        if debug:
            rst["y_dump"] = dma(
                "dbgs",
                ap(dbg["d_y"].ap(), c * 128 * 8 * STR,
                   [[8 * STR, 128], [1, 8 * STR]]),
                m_y[:], waits=[mv])
            dma("dbgs",
                ap(dbg["d_delta"].ap(), c * 128 * 8 * STR,
                   [[8 * STR, 128], [1, 8 * STR]]),
                m_delta[:], waits=[mv])

    def mamba_attn(c):
        S, W = lag(c)
        ev_marks = {}
        grp = None
        for m in range(8):
            g_i = rst["gwout"]
            rst["gwout"] += 2
            s_lo, s_hi = g_i % 2, (g_i + 1) % 2
            mklo = dma(f"wo{s_lo}", w_rout[:, s_lo * 1024:(s_lo + 1) * 1024],
                       ap(cb_wout, m * 128 * 16 * 128,
                          [[16 * 128, 128], [1, 8 * 128]]),
                       waits=[castmark["wout"], rst["woutslot"].get(g_i - 2)])
            mkhi = dma(
                f"wo{s_hi}", w_rout[:, s_hi * 1024:(s_hi + 1) * 1024],
                ap(cb_wout, m * 128 * 16 * 128 + 8 * 128,
                   [[16 * 128, 128], [1, 8 * 128]]),
                waits=[rst["woutslot"].get(g_i + 1 - 2)])
            bank = (4, 5, 6)[m % 3]
            lo_done = None
            for k in range(16):
                rh = m_ybf if k < 8 else m_zbf
                ko = (k % 8) * STR
                s = s_lo if k < 8 else s_hi
                kk = k % 8
                grp = kb.emit(
                    "tensor",
                    lambda e, b=bank, s=s, kk=kk, rh=rh, ko=ko, W=W,
                    k=k: e.matmul(
                        M[b][:, 0:W],
                        w_rout[:, s * 1024 + kk * 128:s * 1024 + kk * 128 +
                               128],
                        rh[:, ko:ko + W], start=(k == 0), stop=(k == 15)),
                    waits=([mklo, mkhi,
                            rst["ybf"][c]] + bwait(bank))
                    if k == 0 else [],
                    inc=(sT if k in (7, 15) else None))
                if k == 7:
                    lo_done = grp
            rst["woutslot"][g_i] = lo_done
            rst["woutslot"][g_i + 1] = grp
            mv = kb.emit("vector",
                         lambda e, b=bank, m=m, W=W: e.tensor_scalar(
                             m_ast[:, m * STR:m * STR + W], M[b][:, 0:W],
                             c_bout[:, m:m + 1], None, ALU.add),
                         waits=[grp, rst["attn_tr_last"]], inc=sV)
            bankfree[bank] = mv
            ev_marks[m] = mv
        rst["attn_mm_done"] = grp
        nts = (W + 127) // 128
        for ts in range(nts):
            tw = min(128, W - ts * 128)
            mxl = dma("xlg",
                      ap(m_xlag, 0, [[DM, tw], [1, DM]]),
                      ap(t_x.ap(), (S + ts * 128) * DM, [[DM, tw], [1, DM]]),
                      waits=[rst["xlag_free"]])
            for mg in range(2):
                bank = 6 + mg
                grp = None
                for q in range(4):
                    m = mg * 4 + q
                    grp = kb.emit(
                        "tensor",
                        lambda e, b=bank, q=q, m=m, ts=ts, tw=tw: e.transpose(
                            M[b][0:tw, q * 128:q * 128 + 128],
                            m_ast[:, m * STR + ts * 128:
                                  m * STR + ts * 128 + tw], c_idn[:]),
                        waits=([ev_marks[mg * 4 + 3]] + bwait(bank))
                        if q == 0 else [],
                        inc=(sT if q == 3 else None))
                rst["attn_tr_last"] = grp
                mv = kb.emit(
                    "vector",
                    lambda e, b=bank, mg=mg, tw=tw: e.tensor_copy(
                        ap(m_ptm, mg * 512, [[DM, tw], [1, 512]]),
                        ap(M[b], 0, [[512, tw], [1, 512]])),
                    waits=[grp, rst["ptm_free"]], inc=sV)
                bankfree[bank] = mv
            mv = kb.emit("gpsimd",
                         lambda e, tw=tw: e.tensor_add(
                             ap(m_ptm, 0, [[DM, tw], [1, DM]]),
                             ap(m_ptm, 0, [[DM, tw], [1, DM]]),
                             ap(m_xlag, 0, [[DM, tw], [1, DM]])),
                         waits=[mxl, mv], inc=sGp)
            rst["xlag_free"] = mv
            mp = dma("ptw",
                     ap(pdram.ap(), (S + ts * 128) * DM, [[DM, tw], [1, DM]]),
                     ap(m_ptm, 0, [[DM, tw], [1, DM]]),
                     waits=[mv])
            rst["partial"][(c, ts)] = mp
            rst["ptm_free"] = mp

    def merge(c):
        S, W = lag(c)
        nts = (W + 127) // 128
        for ts in range(nts):
            tw = min(128, W - ts * 128)
            r0 = S + ts * 128
            f0, f1 = r0 // TCF, (r0 + tw - 1) // TCF
            mp1 = dma("mpt",
                      ap(m_ptm, 0, [[DM, tw], [1, DM]]),
                      ap(pdram.ap(), r0 * DM, [[DM, tw], [1, DM]]),
                      waits=[rst["ptm_free"], rst["partial"][(c, ts)]])
            rst["ptm_free"] = mp1
            mp2 = dma("xlg",
                      ap(m_xlag, 0, [[DM, tw], [1, DM]]),
                      ap(fdram.ap(), r0 * DM, [[DM, tw], [1, DM]]),
                      waits=[rst["xlag_free"]] + rst["fwrit"][f0] +
                      rst["fwrit"][f1])
            mv = kb.emit("gpsimd",
                         lambda e, tw=tw: e.tensor_add(
                             ap(m_ptm, 0, [[DM, tw], [1, DM]]),
                             ap(m_ptm, 0, [[DM, tw], [1, DM]]),
                             ap(m_xlag, 0, [[DM, tw], [1, DM]])),
                         waits=[mp1, mp2], inc=sGp)
            rst["xlag_free"] = mv
            mw = dma("outw",
                     ap(t_out.ap(), r0 * DM, [[DM, tw], [1, DM]]),
                     ap(m_ptm, 0, [[DM, tw], [1, DM]]),
                     waits=[mv])
            rst["ptm_free"] = mw

    def drain(n=1):
        while n > 0 and kb.pend:
            kb.pend.pop(0)()
            n -= 1

    def flush():
        drain(10 ** 9)

    def ffn_expert(f, e):
        sh = (e == 8)
        w1t, w3t, w2t = ((cb_w1s, cb_w3s, cb_w2s) if sh else
                         (cb_w1, cb_w3, cb_w2))
        eoff1 = 0 if sh else e * 32 * 128 * 8 * 128
        eoff2 = 0 if sh else e * 8 * 128 * 32 * 128
        cmk = castmark[("e", e)]
        W1S = ("w1a", "w1b", "w1c", "w1d")
        hid_last = None
        for m in range(32):
            gi1 = rst["gw13"]
            rst["gw13"] += 2
            s1, s3 = gi1 % 4, (gi1 + 1) % 4
            mk1 = dma(W1S[s1], w_r1[:, s1 * 1024:(s1 + 1) * 1024],
                      ap(w1t, eoff1 + m * 128 * 8 * 128,
                         [[8 * 128, 128], [1, 8 * 128]]),
                      waits=[cmk, rst["w1slot"].get(gi1 - 4)])
            mk3 = dma(W1S[s3], w_r1[:, s3 * 1024:(s3 + 1) * 1024],
                      ap(w3t, eoff1 + m * 128 * 8 * 128,
                         [[8 * 128, 128], [1, 8 * 128]]),
                      waits=[rst["w1slot"].get(gi1 + 1 - 4)])
            if debug and f == 0 and e == 0 and m == 0:
                rst["wr1_dump"] = dma("dbgs", dbg["d_wr1"].ap(),
                                      w_r1[:, s1 * 1024:(s1 + 1) * 1024],
                                      waits=[mk1])
            w13m = (mk1, mk3)
            b1, b3 = (0, 1, 4)[m % 3], (2, 3, 5)[m % 3]
            gslot = m % 3
            grp1 = None
            for k in range(8):
                grp1 = kb.emit(
                    "tensor",
                    lambda e_, b=b1, s=s1, k=k: e_.matmul(
                        M[b][:, 0:TCF],
                        w_r1[:, s * 1024 + k * 128:s * 1024 + k * 128 + 128],
                        f_xn[:, k * TCF:(k + 1) * TCF],
                        start=(k == 0), stop=(k == 7)),
                    waits=([w13m[0], w13m[1], rst["xn_ready"],
                            rst.get("wr1_dump")] + bwait(b1))
                    if k == 0 else [],
                    inc=(sT if k == 7 else None))
            grp = None
            for k in range(8):
                grp = kb.emit(
                    "tensor",
                    lambda e_, b=b3, s=s3, k=k: e_.matmul(
                        M[b][:, 0:TCF],
                        w_r1[:, s * 1024 + k * 128:s * 1024 + k * 128 + 128],
                        f_xn[:, k * TCF:(k + 1) * TCF],
                        start=(k == 0), stop=(k == 7)),
                    waits=bwait(b3) if k == 0 else [],
                    inc=(sT if k == 7 else None))
            rst["w1slot"][gi1] = grp
            rst["w1slot"][gi1 + 1] = grp
            if m == 31:
                rst["xn_users"].append(grp)
            ms = kb.emit("scalar",
                         lambda e_, b=b1, gs=gslot: e_.activation(
                             f_gsb[:, gs * TCF:(gs + 1) * TCF],
                             M[b][:, 0:TCF], AF.Silu),
                         waits=[grp1, rst["gsb_free"].get(gslot)],
                         inc=sS)
            if debug and f == 0 and e == 0 and m == 0:
                dma("dbgs", dbg["d_gsb"].ap(), f_gsb[:, 0:TCF], waits=[ms])
                dma("dbgs", dbg["d_xn2"].ap(), f_xn[:], waits=[ms])
            mv = kb.emit("vector",
                         lambda e_, b=b3, gs=gslot, m=m: e_.tensor_tensor(
                             f_hid[:, m * TCF:(m + 1) * TCF],
                             f_gsb[:, gs * TCF:(gs + 1) * TCF],
                             M[b][:, 0:TCF], ALU.mult),
                         waits=[ms, grp, rst.get("hid_dump")], inc=sV)
            bankfree[b1] = mv
            bankfree[b3] = mv
            rst["gsb_free"][gslot] = mv
            hid_last = mv
            drain(2)
        if debug and f == 0 and e == 0:
            rst["hid_dump"] = dma("dbgs", dbg["d_hid"].ap(), f_hid[:],
                                  waits=[hid_last])
        gb_slot = e % 2
        gbm = rst["gb_mark"].get(e)
        for half in range(2):
            grps = {}
            for m4 in range(4):
                m = half * 4 + m4
                gi2 = rst["gw2"]
                rst["gw2"] += 2
                s_lo, s_hi = gi2 % 4, (gi2 + 1) % 4
                mklo = dma(f"w2{'abcd'[s_lo]}",
                           w_r2[:, s_lo * 2048:(s_lo + 1) * 2048],
                           ap(w2t, eoff2 + m * 128 * 32 * 128,
                              [[32 * 128, 128], [1, 16 * 128]]),
                           waits=[cmk, rst["w2slot"].get(gi2 - 4)])
                mkhi = dma(f"w2{'abcd'[s_hi]}",
                           w_r2[:, s_hi * 2048:(s_hi + 1) * 2048],
                           ap(w2t, eoff2 + m * 128 * 32 * 128 + 16 * 128,
                              [[32 * 128, 128], [1, 16 * 128]]),
                           waits=[rst["w2slot"].get(gi2 + 1 - 4)])
                bank = 4 + m4
                grp = None
                lo_done = None
                for k in range(32):
                    s = s_lo if k < 16 else s_hi
                    kk = k % 16
                    grp = kb.emit(
                        "tensor",
                        lambda e_, b=bank, s=s, kk=kk, k=k: e_.matmul(
                            M[b][:, 0:TCF],
                            w_r2[:, s * 2048 + kk * 128:
                                 s * 2048 + kk * 128 + 128],
                            f_hid[:, k * TCF:(k + 1) * TCF],
                            start=(k == 0), stop=(k == 31)),
                        waits=([mklo, mkhi, hid_last] + bwait(bank))
                        if k == 0 else [],
                        inc=(sT if k in (15, 31) else None))
                    if k == 15:
                        lo_done = grp
                grps[m4] = grp
                rst["w2slot"][gi2] = lo_done
                rst["w2slot"][gi2 + 1] = grp
            for m4 in range(4):
                m = half * 4 + m4
                bank = 4 + m4
                d0 = m * TCF
                if e == 0:
                    mv = kb.emit(
                        "vector",
                        lambda e_, b=bank, d0=d0, gs=gb_slot: e_.tensor_tensor(
                            f_acc[:, d0:d0 + TCF], M[b][:, 0:TCF],
                            f_gb[:, gs * TCF:(gs + 1) * TCF], ALU.mult),
                        waits=[grps[m4], gbm, rst["acc_free"],
                               rst["acc_dump"]],
                        inc=sV)
                    bankfree[bank] = mv
                elif sh:
                    mv = kb.emit(
                        "vector",
                        lambda e_, b=bank, d0=d0: e_.scalar_tensor_tensor(
                            f_acc[:, d0:d0 + TCF], M[b][:, 0:TCF], 0.0625,
                            f_acc[:, d0:d0 + TCF], ALU.mult, ALU.add),
                        waits=[grps[m4], rst[("accev", e - 1, m)]], inc=sV)
                    bankfree[bank] = mv
                else:
                    mq = kb.emit(
                        "vector",
                        lambda e_, b=bank, gs=gb_slot: e_.tensor_tensor(
                            f_wt, M[b][:, 0:TCF],
                            f_gb[:, gs * TCF:(gs + 1) * TCF], ALU.mult),
                        waits=[grps[m4], gbm, rst["attn_tr_last"],
                               rst.get("fwt_free")], inc=sV)
                    mv = kb.emit(
                        "gpsimd",
                        lambda e_, d0=d0: e_.tensor_add(
                            f_acc[:, d0:d0 + TCF], f_acc[:, d0:d0 + TCF],
                            f_wt),
                        waits=[mq, rst[("accev", e - 1, m)]], inc=sGp)
                    bankfree[bank] = mq
                    rst["fwt_free"] = mv
                rst[("accev", e, m)] = mv
                drain(2)
        if e < 7:
            nslot = (e + 1) % 2
            rst["gb_mark"][e + 1] = dma(
                f"gb{nslot}",
                ap(f_gb, nslot * TCF, [[2 * TCF, 128], [1, TCF]]),
                ap(gdram.ap(), (e + 1) * TSEQ + f * TCF, [[0, 128], [1, TCF]]),
                waits=[gdram_mark, rst.get(("gbfree", nslot))])
        if not sh:
            rst[("gbfree", gb_slot)] = rst[("accev", e, 7)]

    def ffn_finale(f):
        last_t = None
        fw = []
        for tt in range(4):
            slot = tt % 2
            mvs = []
            for mg in range(2):
                bank = 6 + mg
                grp = None
                for q in range(4):
                    m = mg * 4 + q
                    grp = kb.emit(
                        "tensor",
                        lambda e_, b=bank, q=q, m=m, tt=tt: e_.transpose(
                            M[b][:, q * 128:q * 128 + 128],
                            f_acc[:, m * TCF + tt * 128:
                                  m * TCF + tt * 128 + 128], c_idn[:]),
                        waits=([rst[("accev", 8, 7)]] + bwait(bank))
                        if q == 0 else [],
                        inc=(sT if q == 3 else None))
                last_t = grp
                mv = kb.emit(
                    "vector",
                    lambda e_, b=bank, mg=mg: e_.tensor_copy(
                        ap(f_tm, mg * 512, [[DM, 128], [1, 512]]),
                        ap(M[b], 0, [[512, 128], [1, 512]])),
                    waits=[grp, rst.get(("ftmfree", 0))],
                    inc=sV)
                bankfree[bank] = mv
                mvs.append(mv)
            mw = dma(f"ftw{slot}",
                     ap(fdram.ap(), (f * TCF + tt * 128) * DM,
                        [[DM, 128], [1, DM]]),
                     ap(f_tm, 0, [[DM, 128], [1, DM]]),
                     waits=[mvs[-1]])
            rst[("ftmfree", 0)] = mw
            fw.append(mw)
        rst["fwrit"][f] = fw
        rst["acc_free"] = last_t
        if debug:
            rst["acc_dump"] = dma(
                "dbgs",
                ap(dbg["d_acc"].ap(), f * 128 * 8 * TCF,
                   [[8 * TCF, 128], [1, 8 * TCF]]),
                f_acc[:], waits=[(sV, sV.v)])

    for r in range(max_rounds):
        c = r
        f = r // 2
        ehalf = r % 2
        if 0 <= c - 1 < NCM and (c - 1) in rst["ybf"] and not no_mamba:
            mamba_attn(c - 1)
        if 0 <= c - 2 < NCM and all(
                ff in rst["fwrit"] for ff in
                ((lag(c - 2)[0]) // TCF,
                 (lag(c - 2)[0] + lag(c - 2)[1] - 1) // TCF)):
            merge(c - 2)
        if r >= NCM:
            continue
        if ehalf == 0:
            xn_build(f)
            rst["gb_mark"] = {}
            rst["gb_mark"][0] = dma(
                "gb0", ap(f_gb, 0, [[2 * TCF, 128], [1, TCF]]),
                ap(gdram.ap(), f * TCF, [[0, 128], [1, TCF]]),
                waits=[gdram_mark, rst.get(("gbfree", 0))])
        if not no_mamba:
            mamba_xz_conv(c)
        elist = list(range(0, 5)) if ehalf == 0 else list(range(5, 9))
        for idx, e in enumerate(elist):
            ffn_expert(f, e)
            if no_mamba:
                continue
            if idx == 0:
                mamba_xdbl(c)
            elif idx == 1:
                mamba_delta(c)
            elif idx == 2:
                scan_nq(c, 0)
                scan_nq(c, 1)
            elif idx == 3:
                scan_nq(c, 2)
                scan_nq(c, 3)
                kb.pend.append(lambda c=c: scan_finish(c))
        flush()
        if ehalf == 1:
            ffn_finale(f)

    outc = SQ["outw"][1]
    kb.emit("gpsimd", lambda e: e.engine_nop(), waits=[(outc, outc.v)])

    with nc.Block() as block:
        @block.sync
        def _(eo):
            for fn_ in kb.prog["sync"]:
                fn_(eo)

        @block.gpsimd
        def _(eo):
            for fn_ in kb.prog["gpsimd"]:
                fn_(eo)

        @block.vector
        def _(eo):
            for fn_ in kb.prog["vector"]:
                fn_(eo)

        @block.scalar
        def _(eo):
            for fn_ in kb.prog["scalar"]:
                fn_(eo)

        @block.tensor
        def _(eo):
            for fn_ in kb.prog["tensor"]:
                fn_(eo)

    es.close()
    return nc, kb


def prep_inputs(inp):
    f32 = np.float32
    W_xp = np.asarray(inp["L0_attn_W_xp"], f32)
    wxp = np.zeros((DM, 128), f32)
    wxp[:, 0:64] = W_xp[:, 0:64]
    wxp[:, 64:80] = W_xp[:, 64:80]
    wxp[:, 96:112] = W_xp[:, 80:96]
    A_log = np.asarray(inp["L0_attn_A_log"], f32)
    aprep = np.ascontiguousarray(
        A_log.reshape(8, 128, 16).transpose(1, 0, 2).reshape(128, 128))

    def col8(v):
        return np.ascontiguousarray(np.asarray(v, f32).reshape(8, 128).T)

    def kcol(k):
        return np.ascontiguousarray(
            np.asarray(k, f32).reshape(4, 8, 128).transpose(2, 1, 0)
            .reshape(128, 32))

    shared = {
        "w_in": np.ascontiguousarray(np.asarray(inp["L0_attn_W_in"], f32)),
        "w_out": np.ascontiguousarray(np.asarray(inp["L0_attn_W_out"], f32)),
        "w1e": np.ascontiguousarray(np.asarray(inp["L0_ffn_w1_e"], f32)),
        "w3e": np.ascontiguousarray(np.asarray(inp["L0_ffn_w3_e"], f32)),
        "w2e": np.ascontiguousarray(np.asarray(inp["L0_ffn_w2_e"], f32)),
        "w1s": np.ascontiguousarray(
            np.asarray(inp["L0_ffn_w1_s"], f32).transpose(1, 0, 2)
            .reshape(DM, FF)),
        "w3s": np.ascontiguousarray(
            np.asarray(inp["L0_ffn_w3_s"], f32).transpose(1, 0, 2)
            .reshape(DM, FF)),
        "w2s": np.ascontiguousarray(
            np.asarray(inp["L0_ffn_w2_s"], f32).reshape(FF, DM)),
        "wxp": wxp,
        "wdt": np.ascontiguousarray(np.asarray(inp["L0_attn_W_dt"], f32)),
        "gwt": np.ascontiguousarray(np.asarray(inp["L0_ffn_gate_W"], f32).T),
        "aprep": aprep,
        "dcol": col8(inp["L0_attn_D_param"]),
        "bdtcol": col8(inp["L0_attn_b_dt"]),
        "boutcol": col8(inp["L0_attn_b_out"]),
        "kxcol": kcol(inp["L0_attn_K_x"]),
        "kzcol": kcol(inp["L0_attn_K_z"]),
        "idn": np.eye(128, dtype=f32),
        "cvec": np.broadcast_to(
            np.array([1e-5, 1.0], f32), (128, 2)).copy(),
    }
    x = np.asarray(inp["L0_attn_x"], f32)
    maps = []
    for b in range(8):
        m = dict(shared)
        m["x"] = np.ascontiguousarray(x[b])
        maps.append(m)
    return maps


def kernel(**inputs):
    maps = prep_inputs(inputs)
    nc, kb = build_core(debug=False)
    res = run_bass_kernel_spmd(nc, maps, core_ids=list(range(8)))
    out = np.stack([np.asarray(r["out"], np.float32) for r in res.results])
    return out


# version 09
# speedup vs baseline: 1.2141x; 1.2141x over previous
import numpy as np
import ml_dtypes
from contextlib import ExitStack

import concourse.bass as bass
import concourse.mybir as mybir
from concourse.bass_utils import run_bass_kernel_spmd

F32 = mybir.dt.float32
BF16 = mybir.dt.bfloat16
AF = mybir.ActivationFunctionType
ALU = mybir.AluOpType
AX = mybir.AxisListType

TSEQ, DM, NST, NEXP, FF = 4096, 1024, 16, 8, 4096
TCF, NCF = 512, 8
TCM, NCM = 256, 16
STR = 260
NQ = 4
ENGS = ("sync", "gpsimd", "vector", "scalar", "tensor")


def lag(c):
    if c == 0:
        return 0, TCM - 2
    return TCM * c - 2, (TCM + 2 if c == NCM - 1 else TCM)


class Ctr:
    def __init__(self, name):
        self.name = name
        self.v = 0
        self.sem = None


def ap(t, off, dims):
    if isinstance(t, bass.AP):
        t = t.tensor
    return bass.AP(t, off, [list(d) for d in dims])


class KB:
    def __init__(self, nc):
        self.nc = nc
        self.prog = {e: [] for e in ENGS}
        self.pend = []
        self.last_wait = {e: {} for e in ENGS}
        self.n_instr = 0

    def emit(self, eng, fn, waits=(), inc=None, inc_n=1):
        todo = []
        for w in waits:
            if w is None or w == 0:
                continue
            c, v = w
            if v is None or v <= 0:
                continue
            assert v <= c.v, f"wait on future value of {c.name}: {v} > {c.v}"
            lw = self.last_wait[eng].get(c.name, 0)
            if v > lw:
                todo.append((c, v))
                self.last_wait[eng][c.name] = v
        mark = None
        if inc is not None:
            inc.v += inc_n
            mark = (inc, inc.v)
        self.n_instr += 1 + len(todo)

        def replay(eo, todo=todo, fn=fn, inc=inc, n=inc_n):
            for (c, v) in todo:
                eo.wait_ge(c.sem, v)
            ins = fn(eo)
            if inc is not None:
                ins.then_inc(inc.sem, n)
        self.prog[eng].append(replay)
        return mark


def build_core(debug=False, max_rounds=NCM + 2, no_mamba=False):
    nc = bass.Bass("TRN2", target_bir_lowering=False)
    kb = KB(nc)
    es = ExitStack()

    def din(name, shape, dt=F32):
        return nc.dram_tensor(name, list(shape), dt, kind="ExternalInput")

    t_x = din("x", [TSEQ, DM])
    t_win = din("w_in", [DM, 2 * DM])
    t_wout = din("w_out", [2 * DM, DM])
    t_w1e = din("w1e", [NEXP, DM, FF])
    t_w3e = din("w3e", [NEXP, DM, FF])
    t_w2e = din("w2e", [NEXP, FF, DM])
    t_w1s = din("w1s", [DM, FF])
    t_w3s = din("w3s", [DM, FF])
    t_w2s = din("w2s", [FF, DM])
    t_wxp = din("wxp", [DM, 128])
    t_wdt = din("wdt", [64, DM])
    t_gwt = din("gwt", [DM, 8])
    t_aprep = din("aprep", [128, 128])
    t_dcol = din("dcol", [128, 8])
    t_bdtcol = din("bdtcol", [128, 8])
    t_boutcol = din("boutcol", [128, 8])
    t_kxcol = din("kxcol", [128, 32])
    t_kzcol = din("kzcol", [128, 32])
    t_idn = din("idn", [128, 128])
    t_cvec = din("cvec", [128, 2])

    t_out = nc.dram_tensor("out", [TSEQ, DM], F32, kind="ExternalOutput")

    def dint(name, numel, dt=BF16):
        return nc.dram_tensor(name, [numel], dt, kind="Internal")

    cb_win = dint("cb_win", 16 * 128 * 8 * 128)
    cb_wout = dint("cb_wout", 8 * 128 * 16 * 128)
    cb_w1 = dint("cb_w1", NEXP * 32 * 128 * 8 * 128)
    cb_w3 = dint("cb_w3", NEXP * 32 * 128 * 8 * 128)
    cb_w2 = dint("cb_w2", NEXP * 8 * 128 * 32 * 128)
    cb_w1s = dint("cb_w1s", 32 * 128 * 8 * 128)
    cb_w3s = dint("cb_w3s", 32 * 128 * 8 * 128)
    cb_w2s = dint("cb_w2s", 8 * 128 * 32 * 128)
    bdram = nc.dram_tensor("bdram", [NST, TSEQ], F32, kind="Internal")
    cdram = nc.dram_tensor("cdram", [NST, TSEQ], F32, kind="Internal")
    gdram = nc.dram_tensor("gdram", [8, TSEQ], F32, kind="Internal")
    fdram = nc.dram_tensor("fdram", [TSEQ, DM], F32, kind="Internal")
    pdram = nc.dram_tensor("pdram", [TSEQ, DM], F32, kind="Internal")

    dbg = {}
    if debug:
        dbg["d_xs"] = nc.dram_tensor("d_xs", [NCM, 128, 8 * STR], F32,
                                     kind="ExternalOutput")
        dbg["d_delta"] = nc.dram_tensor("d_delta", [NCM, 128, 8 * STR], F32,
                                        kind="ExternalOutput")
        dbg["d_y"] = nc.dram_tensor("d_y", [NCM, 128, 8 * STR], F32,
                                    kind="ExternalOutput")
        dbg["d_acc"] = nc.dram_tensor("d_acc", [NCF, 128, 8 * TCF], F32,
                                      kind="ExternalOutput")
        dbg["d_gates"] = nc.dram_tensor("d_gates", [8, TSEQ], F32,
                                        kind="ExternalOutput")
        dbg["d_xn"] = nc.dram_tensor("d_xn", [NCF, 128, 8 * TCF], BF16,
                                     kind="ExternalOutput")
        dbg["d_hid"] = nc.dram_tensor("d_hid", [128, 32 * TCF], BF16,
                                      kind="ExternalOutput")
        dbg["d_cbw1"] = nc.dram_tensor("d_cbw1", [32 * 128 * 8 * 128], BF16,
                                       kind="ExternalOutput")
        dbg["d_xn2"] = nc.dram_tensor("d_xn2", [128, 8 * TCF], BF16,
                                      kind="ExternalOutput")
        dbg["d_gsb"] = nc.dram_tensor("d_gsb", [128, TCF], F32,
                                      kind="ExternalOutput")
        dbg["d_wr1"] = nc.dram_tensor("d_wr1", [128, 1024], BF16,
                                      kind="ExternalOutput")

    def sb(name, cols, dt=F32, parts=128):
        return es.enter_context(nc.sbuf_tensor(name, [parts, cols], dt))

    c_aneg = sb("c_aneg", 128)
    c_d = sb("c_d", 8)
    c_bdt = sb("c_bdt", 8)
    c_bout = sb("c_bout", 8)
    c_kx = sb("c_kx", 32)
    c_kz = sb("c_kz", 32)
    c_idn = sb("c_idn", 128)
    c_wxp = sb("c_wxp", 8 * 128, BF16)
    c_wdt = sb("c_wdt", DM, BF16, parts=64)
    c_ghi = sb("c_ghi", 64, BF16)
    c_glo = sb("c_glo", 64, BF16)
    c_cv = sb("c_cv", 2)
    c_mneg = sb("c_mneg", 32)
    c_rs = sb("c_rs", 32)

    m_ext = sb("m_ext", 264)
    m_cpre = sb("m_cpre", 2 * STR)
    m_carry = sb("m_carry", 48)
    m_xsconv = sb("m_xsconv", 8 * STR)
    m_xsbf = sb("m_xsbf", 8 * STR, BF16)
    m_zbf = sb("m_zbf", 8 * STR, BF16)
    m_dtlow = sb("m_dtlow", STR, BF16, parts=64)
    m_bc = sb("m_bc", STR)
    m_delta = sb("m_delta", 8 * STR)
    m_dtu = sb("m_dtu", 8 * STR)
    m_dA = sb("m_dA", 2 * NQ * STR)
    m_dBu = sb("m_dBu", NQ * STR)
    m_h = m_dBu
    m_brep = sb("m_brep", NQ * STR)
    m_crep = sb("m_crep", NQ * STR)
    m_ytmp = sb("m_ytmp", STR)
    m_y = sb("m_y", 8 * STR)
    m_ybf = sb("m_ybf", 8 * STR, BF16)
    m_hst = sb("m_hst", 128)
    m_ast = sb("m_ast", 8 * STR)
    m_ptm = sb("m_ptm", DM)
    m_xlag = sb("m_xlag", DM)

    f_x = sb("f_x", 2 * DM)
    f_xntm = sb("f_xntm", DM)
    f_xn = sb("f_xn", 8 * TCF, BF16)
    f_hid = sb("f_hid", 32 * TCF, BF16)
    f_acc = sb("f_acc", 8 * TCF)
    f_gb = sb("f_gb", 2 * TCF)
    f_gsb = sb("f_gsb", 3 * TCF)
    f_tm = sb("f_tm", DM)
    w_r1 = sb("w_r1", 4 * 1024, BF16)
    w_r2 = sb("w_r2", 4 * 2048, BF16)
    w_rin = sb("w_rin", 2 * 1024, BF16)
    w_rout = sb("w_rout", 2 * 1024, BF16)
    g_pad = m_ext
    g_gt = m_cpre
    g_sm = m_dA
    f_wt = ap(m_ast, 0, [[8 * STR, 128], [1, TCF]])

    M = [es.enter_context(nc.psum_tensor(f"pm{i}", [128, 512], F32))
         for i in range(8)]

    sV, sS, sT, sGp = Ctr("sV"), Ctr("sS"), Ctr("sT"), Ctr("sGp")
    stream_defs = [
        ("cst", "sync"),
        ("win0", "sync"), ("win1", "sync"),
        ("wo0", "sync"), ("wo1", "sync"),
        ("w1a", "sync"), ("w1b", "sync"), ("w1c", "sync"), ("w1d", "sync"),
        ("w2a", "sync"), ("w2b", "sync"), ("w2c", "sync"), ("w2d", "sync"),
        ("fx0", "scalar"), ("fx1", "scalar"),
        ("bcw", "scalar"), ("gdw", "scalar"),
        ("brb", "scalar"), ("crb", "scalar"),
        ("gb0", "scalar"), ("gb1", "scalar"),
        ("ptw", "scalar"), ("ftw0", "scalar"), ("ftw1", "scalar"),
        ("outw", "scalar"), ("xlg", "scalar"), ("mpt", "scalar"),
        ("dbgs", "scalar"),
        ("cwin", "gpsimd"), ("cwout", "gpsimd"),
    ] + [(f"ce{e}", "gpsimd") for e in range(9)]
    SQ = {}
    for name, q in stream_defs:
        SQ[name] = (q, Ctr(name))
    for c in (sV, sS, sT, sGp):
        c.sem = es.enter_context(nc.semaphore(c.name))
    for name, (q, c) in SQ.items():
        c.sem = es.enter_context(nc.semaphore(name))

    def dma(stream, out, in_, waits=(), **kw):
        q, c = SQ[stream]
        return kb.emit(q, lambda e: e.dma_start(out=out, in_=in_, **kw),
                       waits, inc=c, inc_n=16)

    bankfree = {i: None for i in range(8)}

    def bwait(i):
        return [bankfree[i]]

    castmark = {}

    def cast_cb(stream, dst, src, D_, F_, off_src, off_dst, ksplit=8):
        nk, nm = D_ // 128, F_ // 128
        mk = None
        for m in range(nm):
            for k0 in range(0, nk, ksplit):
                ks = min(ksplit, nk - k0)
                mk = dma(
                    stream,
                    ap(dst, off_dst + m * (128 * nk * 128) + k0 * 128,
                       [[nk * 128, 128], [128, ks], [1, 128]]),
                    ap(src, off_src + (k0 * 128) * F_ + m * 128,
                       [[F_, 128], [128 * F_, ks], [1, 128]]))
        return mk

    castmark["win"] = cast_cb("cwin", cb_win, t_win.ap(), DM, 2 * DM, 0, 0)
    castmark["wout"] = cast_cb("cwout", cb_wout, t_wout.ap(), 2 * DM, DM, 0, 0)
    for e in range(NEXP):
        st = f"ce{e}"
        cast_cb(st, cb_w1, t_w1e.ap(), DM, FF, e * DM * FF,
                e * 32 * 128 * 8 * 128)
        cast_cb(st, cb_w3, t_w3e.ap(), DM, FF, e * DM * FF,
                e * 32 * 128 * 8 * 128)
        castmark[("e", e)] = cast_cb(st, cb_w2, t_w2e.ap(), FF, DM,
                                     e * FF * DM, e * 8 * 128 * 32 * 128)
    cast_cb("ce8", cb_w1s, t_w1s.ap(), DM, FF, 0, 0)
    cast_cb("ce8", cb_w3s, t_w3s.ap(), DM, FF, 0, 0)
    castmark[("e", 8)] = cast_cb("ce8", cb_w2s, t_w2s.ap(), FF, DM, 0, 0)

    mk_ap = dma("cst", c_aneg[:], t_aprep.ap())
    dma("cst", c_d[:], t_dcol.ap())
    dma("cst", c_bdt[:], t_bdtcol.ap())
    dma("cst", c_bout[:], t_boutcol.ap())
    dma("cst", c_kx[:], t_kxcol.ap())
    dma("cst", c_kz[:], t_kzcol.ap())
    mk_id = dma("cst", c_idn[:], t_idn.ap())
    dma("cst", c_cv[:], t_cvec.ap())
    for k in range(8):
        mw = dma("cst", f_xntm[:, 0:128],
                 ap(t_wxp.ap(), k * 128 * 128, [[128, 128], [1, 128]]),
                 waits=[(sV, sV.v)])
        kb.emit("vector",
                lambda e, k=k: e.tensor_copy(c_wxp[:, k * 128:(k + 1) * 128],
                                             f_xntm[:, 0:128]),
                waits=[mw], inc=sV)
    mw = dma("cst", ap(m_ast, 0, [[8 * STR, 64], [1, DM]]),
             ap(t_wdt.ap(), 0, [[DM, 64], [1, DM]]))
    kb.emit("vector",
            lambda e: e.tensor_copy(c_wdt[0:64, :],
                                    ap(m_ast, 0, [[8 * STR, 64], [1, DM]])),
            waits=[mw], inc=sV)
    for k in range(8):
        mw = dma("cst", g_pad[:, 0:8],
                 ap(t_gwt.ap(), k * 128 * 8, [[8, 128], [1, 8]]),
                 waits=[(sV, sV.v)])
        mgh = kb.emit("vector",
                      lambda e, k=k: e.tensor_copy(c_ghi[:, k * 8:(k + 1) * 8],
                                                   g_pad[:, 0:8]),
                      waits=[mw], inc=sV)
        kb.emit("vector",
                lambda e, k=k: e.tensor_tensor(
                    c_glo[:, k * 8:(k + 1) * 8], g_pad[:, 0:8],
                    c_ghi[:, k * 8:(k + 1) * 8], ALU.subtract),
                waits=[mgh], inc=sV)
    mk_cv0 = (SQ["cst"][1], SQ["cst"][1].v)
    ms = kb.emit("scalar", lambda e: e.activation(c_aneg[:], c_aneg[:], AF.Exp),
                 waits=[mk_ap, mk_cv0], inc=sS)
    kb.emit("vector", lambda e: e.tensor_scalar_mul(c_aneg[:], c_aneg[:], -1.0),
            waits=[ms], inc=sV)
    kb.emit("vector", lambda e: e.memset(m_carry[:], 0.0), inc=sV)
    kb.emit("vector", lambda e: e.memset(m_hst[:], 0.0), inc=sV)
    kb.emit("vector", lambda e: e.memset(g_pad[:], 0.0), inc=sV)
    kb.emit("vector", lambda e: e.memset(g_gt[:], 0.0), inc=sV)
    consts_v = (sV, sV.v)
    mk_cv = (SQ["cst"][1], SQ["cst"][1].v)

    norm_done = {}
    tile_trans_done = None
    for i in range(32):
        slot = i % 2
        mx = dma(f"fx{slot}", f_x[:, slot * DM:(slot + 1) * DM],
                 ap(t_x.ap(), i * 128 * DM, [[DM, 128], [1, DM]]),
                 waits=[norm_done.get(i - 2)])
        mrd = kb.emit("vector",
                      lambda e, s=slot: e.reduce_sum(
                          m_ytmp[:, 0:1], f_x[:, s * DM:(s + 1) * DM], AX.X),
                      waits=[mx, consts_v, mk_cv], inc=sV)
        mv = kb.emit("vector",
                     lambda e, i=i: e.tensor_scalar_mul(
                         c_mneg[:, i:i + 1], m_ytmp[:, 0:1], -1.0 / DM),
                     waits=[mrd], inc=sV)
        ms = kb.emit("scalar",
                     lambda e, s=slot, i=i: e.activation(
                         f_acc[:, 0:DM], f_x[:, s * DM:(s + 1) * DM],
                         AF.Square, bias=c_mneg[:, i:i + 1],
                         accum_out=c_rs[:, i:i + 1]),
                     waits=[mv], inc=sS)
        ms = kb.emit("scalar",
                     lambda e, i=i: e.activation(c_rs[:, i:i + 1],
                                                 c_rs[:, i:i + 1],
                                                 AF.Ln, bias=c_cv[:, 0:1],
                                                 scale=1.0 / DM),
                     waits=[ms], inc=sS)
        ms = kb.emit("scalar",
                     lambda e, i=i: e.activation(
                         c_rs[:, i:i + 1], c_rs[:, i:i + 1], AF.Exp,
                         scale=-0.5),
                     waits=[ms], inc=sS)
        mv = kb.emit("vector",
                     lambda e, s=slot, i=i: e.tensor_scalar(
                         f_xntm[:], f_x[:, s * DM:(s + 1) * DM],
                         c_mneg[:, i:i + 1], c_rs[:, i:i + 1],
                         ALU.add, ALU.mult),
                     waits=[ms, tile_trans_done], inc=sV)
        norm_done[i] = mv
        mhi = kb.emit("vector",
                      lambda e: e.tensor_copy(m_xsbf[:, 0:DM], f_xntm[:]),
                      waits=[mv], inc=sV)
        mlo = kb.emit("vector",
                      lambda e: e.tensor_tensor(m_xsconv[:, 0:DM], f_xntm[:],
                                                m_xsbf[:, 0:DM], ALU.subtract),
                      waits=[mhi], inc=sV)
        last_ev = None
        for which, src in ((0, f_xntm), (1, m_xsconv)):
            for half in range(2):
                bank = 6 + half
                grp = None
                for q in range(4):
                    d = half * 4 + q
                    grp = kb.emit(
                        "tensor",
                        lambda e, b=bank, q=q, src=src, d=d: e.transpose(
                            M[b][:, q * 128:(q + 1) * 128],
                            src[:, d * 128:d * 128 + 128], c_idn[:]),
                        waits=([mlo, mk_id] + bwait(bank)) if q == 0 else [],
                        inc=(sT if q == 3 else None))
                dst0 = which * 1024 + half * 512
                mse = kb.emit(
                    "scalar",
                    lambda e, b=bank, d0=dst0: e.copy(
                        f_xn[:, d0:d0 + 512], M[b][:, 0:512]),
                    waits=[grp], inc=sS)
                bankfree[bank] = mse
                last_ev = mse
        nmm = 0
        grp = None
        for (lh, rh0) in ((c_ghi, 0), (c_glo, 0), (c_ghi, 1024)):
            for k in range(8):
                st0, sp0 = (nmm == 0), (nmm == 23)
                grp = kb.emit(
                    "tensor",
                    lambda e, lh=lh, k=k, rh0=rh0, st0=st0, sp0=sp0: e.matmul(
                        M[4][0:8, 0:128], lh[:, k * 8:(k + 1) * 8],
                        f_xn[:, rh0 + k * 128:rh0 + k * 128 + 128],
                        start=st0, stop=sp0),
                    waits=([last_ev, consts_v] + bwait(4))
                    if nmm == 0 else [],
                    inc=(sT if sp0 else None))
                nmm += 1
        mv = kb.emit("vector",
                     lambda e: e.tensor_copy(g_pad[0:8, 0:128],
                                             M[4][0:8, 0:128]),
                     waits=[grp], inc=sV)
        bankfree[4] = mv
        mt = kb.emit("tensor",
                     lambda e: e.transpose(M[5][:, 0:128], g_pad[:, 0:128],
                                           c_idn[:]),
                     waits=[mv] + bwait(5), inc=sT)
        mv = kb.emit("vector",
                     lambda e: e.tensor_copy(g_sm[:, 0:8], M[5][:, 0:8]),
                     waits=[mt], inc=sV)
        bankfree[5] = mv
        kb.emit("vector",
                lambda e: e.reduce_max(g_sm[:, 40:41], g_sm[:, 0:8], AX.X,
                                       negate=True), waits=[mv], inc=sV)
        ms = kb.emit("scalar",
                     lambda e: e.activation(g_sm[:, 8:16], g_sm[:, 0:8],
                                            AF.Exp, bias=g_sm[:, 40:41]),
                     waits=[(sV, sV.v)], inc=sS)
        mq = kb.emit("vector",
                     lambda e: e.reduce_sum(g_sm[:, 41:42], g_sm[:, 8:16],
                                            AX.X),
                     waits=[ms], inc=sV)
        mq = kb.emit("vector",
                     lambda e: e.reciprocal(g_sm[:, 42:43], g_sm[:, 41:42]),
                     waits=[mq], inc=sV)
        mq = kb.emit("vector",
                     lambda e: e.tensor_scalar_mul(g_sm[:, 0:8], g_sm[:, 8:16],
                                                   g_sm[:, 42:43]),
                     waits=[mq], inc=sV)
        mq = kb.emit("vector",
                     lambda e: e.tensor_single_scalar(g_sm[:, 8:16],
                                                      g_sm[:, 0:8],
                                                      g_sm[:, 42:43],
                                                      ALU.is_ge),
                     waits=[mq], inc=sV)
        mq = kb.emit("vector",
                     lambda e: e.scalar_tensor_tensor(g_sm[:, 16:24],
                                                      g_sm[:, 8:16],
                                                      -2.0, g_sm[:, 0:8],
                                                      ALU.mult, ALU.add),
                     waits=[mq], inc=sV)
        mq = kb.emit("vector",
                     lambda e: e.reduce_max(g_sm[:, 43:44], g_sm[:, 16:24],
                                            AX.X),
                     waits=[mq], inc=sV)
        mq = kb.emit("vector",
                     lambda e: e.tensor_single_scalar(g_sm[:, 8:16],
                                                      g_sm[:, 0:8],
                                                      g_sm[:, 43:44],
                                                      ALU.is_ge),
                     waits=[mq], inc=sV)
        mq = kb.emit("vector",
                     lambda e: e.scalar_tensor_tensor(g_sm[:, 16:24],
                                                      g_sm[:, 0:8],
                                                      0.5, g_sm[:, 8:16],
                                                      ALU.mult, ALU.mult),
                     waits=[mq], inc=sV)
        mv = kb.emit("vector",
                     lambda e: e.tensor_copy(g_gt[:, 0:8], g_sm[:, 16:24]),
                     waits=[mq], inc=sV)
        mt = kb.emit("tensor",
                     lambda e: e.transpose(M[5][:, 0:128], g_gt[:, 0:128],
                                           c_idn[:]),
                     waits=[mv] + bwait(5), inc=sT)
        mv = kb.emit("vector",
                     lambda e: e.tensor_copy(g_pad[0:8, 0:128],
                                             M[5][0:8, 0:128]),
                     waits=[mt], inc=sV)
        bankfree[5] = mv
        dma("gdw", ap(gdram.ap(), i * 128, [[TSEQ, 8], [1, 128]]),
            g_pad[0:8, 0:128], waits=[mv])
        tile_trans_done = (sT, sT.v)
    gdram_mark = (SQ["gdw"][1], SQ["gdw"][1].v)
    if debug:
        dma("dbgs", dbg["d_gates"].ap(), gdram.ap(), waits=[gdram_mark])
        dma("dbgs", dbg["d_cbw1"].ap(),
            ap(cb_w1, 0, [[131072, 32], [512, 256], [1, 512]]),
            waits=[castmark[("e", 0)]])

    rst = {
        "xn_ready": None, "xn_users": [], "xn_tr_last": None,
        "ybf": {}, "attn_mm_done": None, "attn_tr_last": None,
        "partial": {}, "fwrit": {},
        "acc_free": None, "gb_mark": {},
        "gsb_free": {}, "dAslot_free": {},
        "brep_free": None, "gwin": 0, "gw13": 0, "gw2": 0, "gwout": 0,
        "winfree": {}, "w1slot": {}, "w2slot": {}, "woutslot": {},
        "delta_mark": None, "dtu_done": None, "bc_mark": None,
        "delta_mm_last": None, "exp_marks": {}, "xlag_free": None,
        "ptm_free": None, "ybf_all": None,
        "xs_dump": None, "y_dump": None, "acc_dump": None, "xn_dump": None,
        "hid_dump": None, "wr1_dump": None, "fwt_free": None,
    }

    def xn_build(f):
        guard = rst["xn_users"]
        rst["xn_users"] = []
        for i4 in range(4):
            gi = 4 * f + i4
            slot = i4 % 2
            key = ("r", f, i4 - 2) if i4 >= 2 else ("r", f - 1, i4 + 2)
            prevn = norm_done.get(key, norm_done.get(31) if f == 0 else None)
            mx = dma(f"fx{slot}", f_x[:, slot * DM:(slot + 1) * DM],
                     ap(t_x.ap(), gi * 128 * DM, [[DM, 128], [1, DM]]),
                     waits=[prevn])
            mv = kb.emit("vector",
                         lambda e, s=slot, gi=gi: e.tensor_scalar(
                             f_xntm[:], f_x[:, s * DM:(s + 1) * DM],
                             c_mneg[:, gi:gi + 1], c_rs[:, gi:gi + 1],
                             ALU.add, ALU.mult),
                         waits=[mx, rst["xn_tr_last"]] + guard, inc=sV)
            guard = []
            norm_done[("r", f, i4)] = mv
            for half in range(2):
                bank = 6 + half
                grp = None
                for q in range(4):
                    d = half * 4 + q
                    grp = kb.emit(
                        "tensor",
                        lambda e, b=bank, q=q, d=d: e.transpose(
                            M[b][:, q * 128:(q + 1) * 128],
                            f_xntm[:, d * 128:d * 128 + 128], c_idn[:]),
                        waits=([mv] + bwait(bank)) if q == 0 else [],
                        inc=(sT if q == 3 else None))
                rst["xn_tr_last"] = grp
                mse = kb.emit(
                    "scalar",
                    lambda e, b=bank, h=half, i4=i4: e.copy(
                        ap(f_xn, (h * 4) * TCF + i4 * 128,
                           [[8 * TCF, 128], [TCF, 4], [1, 128]]),
                        ap(M[b], 0, [[512, 128], [128, 4], [1, 128]])),
                    waits=[grp, rst["xn_dump"]], inc=sS)
                bankfree[bank] = mse
                rst["xn_ready"] = mse
        if debug:
            rst["xn_dump"] = dma(
                "dbgs",
                ap(dbg["d_xn"].ap(), f * 128 * 8 * TCF,
                   [[8 * TCF, 128], [1, 8 * TCF]]),
                f_xn[:], waits=[rst["xn_ready"]])

    def mamba_xz_conv(c):
        S, W = lag(c)
        off = (c % 2) * TCM
        silu_marks = {}
        for j in range(16):
            g_i = rst["gwin"]
            rst["gwin"] += 1
            slot = g_i % 2
            wmk = dma(
                f"win{slot}", w_rin[:, slot * 1024:(slot + 1) * 1024],
                ap(cb_win, j * 128 * 8 * 128, [[8 * 128, 128], [1, 8 * 128]]),
                waits=[castmark["win"], rst["winfree"].get(g_i - 2)])
            bank = (4, 5, 6)[j % 3]
            grp = None
            for k in range(8):
                grp = kb.emit(
                    "tensor",
                    lambda e, b=bank, s=slot, k=k, off=off: e.matmul(
                        M[b][:, 0:TCM],
                        w_rin[:, s * 1024 + k * 128:s * 1024 + k * 128 + 128],
                        f_xn[:, k * TCF + off:k * TCF + off + TCM],
                        start=(k == 0), stop=(k == 7)),
                    waits=([wmk, rst["xn_ready"]] + bwait(bank))
                    if k == 0 else [],
                    inc=(sT if k == 7 else None))
            rst["winfree"][g_i] = grp
            if j == 15:
                rst["xn_users"].append(grp)
            kb.emit("vector",
                    lambda e, j=j: e.tensor_copy(m_ext[:, 0:3],
                                                 m_carry[:, j * 3:j * 3 + 3]),
                    waits=[silu_marks.get(j - 2), rst["delta_mark"]])
            mv = kb.emit("vector",
                         lambda e, b=bank: e.tensor_copy(m_ext[:, 3:3 + TCM],
                                                         M[b][:, 0:TCM]),
                         waits=[grp], inc=sV)
            bankfree[bank] = mv
            if c == NCM - 1:
                kb.emit("vector",
                        lambda e: e.memset(m_ext[:, 3 + TCM:3 + TCM + 2], 0.0))
            cs = (j % 2) * STR
            kc = c_kx if j < 8 else c_kz
            g = j % 8
            eo = 2 if c == 0 else 0
            mq = kb.emit("vector",
                         lambda e, cs=cs, kc=kc, g=g, eo=eo, W=W:
                         e.tensor_scalar_mul(m_cpre[:, cs:cs + W],
                                             m_ext[:, eo:eo + W],
                                             kc[:, g * 4:g * 4 + 1]),
                         waits=[mv], inc=sV)
            for w in range(1, 4):
                mq = kb.emit("vector",
                             lambda e, cs=cs, kc=kc, g=g, w=w, eo=eo, W=W:
                             e.scalar_tensor_tensor(
                                 m_cpre[:, cs:cs + W],
                                 m_ext[:, eo + w:eo + w + W],
                                 kc[:, g * 4 + w:g * 4 + w + 1],
                                 m_cpre[:, cs:cs + W], ALU.mult, ALU.add),
                             waits=[mq], inc=sV)
            mv = kb.emit("vector",
                         lambda e, j=j: e.tensor_copy(
                             m_carry[:, j * 3:j * 3 + 3],
                             m_ext[:, TCM:TCM + 3]),
                         inc=sV)
            if j < 8:
                kb.emit("scalar",
                        lambda e, cs=cs, g=g, W=W: e.activation(
                            m_xsconv[:, g * STR:g * STR + W],
                            m_cpre[:, cs:cs + W], AF.Silu),
                        waits=[mv, rst["attn_mm_done"], rst["ybf_all"],
                               rst["xs_dump"]])
                sm = kb.emit("scalar",
                             lambda e, cs=cs, g=g, W=W: e.activation(
                                 m_xsbf[:, g * STR:g * STR + W],
                                 m_cpre[:, cs:cs + W], AF.Silu),
                             inc=sS)
            else:
                sm = kb.emit("scalar",
                             lambda e, cs=cs, g=g, W=W: e.activation(
                                 m_zbf[:, g * STR:g * STR + W],
                                 m_cpre[:, cs:cs + W], AF.Silu),
                             waits=[mv, rst["attn_mm_done"]],
                             inc=sS)
            silu_marks[j] = sm
        rst["xs_ready"] = silu_marks[7]
        rst["z_ready"] = silu_marks[15]
        if debug:
            rst["xs_dump"] = dma(
                "dbgs",
                ap(dbg["d_xs"].ap(), c * 128 * 8 * STR,
                   [[8 * STR, 128], [1, 8 * STR]]),
                m_xsconv[:], waits=[rst["xs_ready"]])

    def mamba_xdbl(c):
        S, W = lag(c)
        grp = None
        for k in range(8):
            grp = kb.emit(
                "tensor",
                lambda e, k=k, W=W: e.matmul(
                    M[6][:, 0:W], c_wxp[:, k * 128:(k + 1) * 128],
                    m_xsbf[:, k * STR:k * STR + W],
                    start=(k == 0), stop=(k == 7)),
                waits=([rst["xs_ready"], consts_v] + bwait(6))
                if k == 0 else [],
                inc=(sT if k == 7 else None))
        mv = kb.emit("vector",
                     lambda e, W=W: e.tensor_copy(
                         ap(m_bc, 64 * STR, [[STR, 48], [1, W]]),
                         ap(M[6], 64 * 512, [[512, 48], [1, W]])),
                     waits=[grp, rst["bc_mark"]], inc=sV)
        ms = kb.emit("scalar",
                     lambda e, W=W: e.copy(m_dtlow[0:64, 0:W],
                                           M[6][0:64, 0:W]),
                     waits=[grp, rst["delta_mm_last"]], inc=sS)
        bankfree[6] = ms
        rst["dtlow_mark"] = ms
        dma("bcw", ap(bdram.ap(), S, [[TSEQ, 16], [1, W]]),
            ap(m_bc, 64 * STR, [[STR, 16], [1, W]]),
            waits=[mv, rst["brep_free"]])
        rst["bc_mark"] = dma("bcw",
                             ap(cdram.ap(), S, [[TSEQ, 16], [1, W]]),
                             ap(m_bc, 96 * STR, [[STR, 16], [1, W]]))

    def mamba_delta(c):
        S, W = lag(c)
        last_ln = None
        for g in range(8):
            bank = 6 + g % 2
            mm = kb.emit(
                "tensor",
                lambda e, b=bank, g=g, W=W: e.matmul(
                    M[b][:, 0:W], c_wdt[0:64, g * 128:(g + 1) * 128],
                    m_dtlow[0:64, 0:W], start=True, stop=True),
                waits=[rst["dtlow_mark"], consts_v] + bwait(bank), inc=sT)
            rst["delta_mm_last"] = mm
            cs = (g % 2) * STR
            mexp = kb.emit("scalar",
                           lambda e, b=bank, cs=cs, g=g, W=W: e.activation(
                               m_cpre[:, cs:cs + W], M[b][:, 0:W], AF.Exp,
                               bias=c_bdt[:, g:g + 1]),
                           waits=[mm, rst["dtu_done"]], inc=sS)
            last_ln = kb.emit("scalar",
                              lambda e, cs=cs, g=g, W=W: e.activation(
                                  m_delta[:, g * STR:g * STR + W],
                                  m_cpre[:, cs:cs + W], AF.Ln,
                                  bias=c_cv[:, 1:2]),
                              waits=[mexp], inc=sS)
            bankfree[bank] = last_ln
        rst["delta_mark"] = last_ln
        rst["exp_marks"] = {}

    def scan_nq(c, nq):
        S, W = lag(c)
        for g in range(8):
            par = (nq * 8 + g) % 2
            base = par * NQ * STR
            mk = None
            for nl in range(NQ):
                n = nq * NQ + nl
                mk = kb.emit(
                    "scalar",
                    lambda e, b=base, nl=nl, g=g, n=n, W=W: e.activation(
                        m_dA[:, b + nl * STR:b + nl * STR + W],
                        m_delta[:, g * STR:g * STR + W], AF.Exp,
                        scale=c_aneg[:, g * 16 + n:g * 16 + n + 1]),
                    waits=[rst["dAslot_free"].get(par), rst["delta_mark"],
                           consts_v]
                    if nl == 0 else [],
                    inc=(sS if nl == NQ - 1 else None))
            rst["exp_marks"][(nq, g)] = mk

        def q_bcast(nq=nq, S=S, W=W):
            dma("brb",
                ap(m_brep, 0, [[NQ * STR, 128], [STR, NQ], [1, W]]),
                ap(bdram.ap(), nq * NQ * TSEQ + S,
                   [[0, 128], [TSEQ, NQ], [1, W]]),
                waits=[rst["bc_mark"], rst["brep_free"]])
            rst["mcc"] = dma("crb",
                             ap(m_crep, 0, [[NQ * STR, 128], [STR, NQ],
                                            [1, W]]),
                             ap(cdram.ap(), nq * NQ * TSEQ + S,
                                [[0, 128], [TSEQ, NQ], [1, W]]))
            rst["mbb"] = (SQ["brb"][1], SQ["brb"][1].v)
        kb.pend.append(q_bcast)
        for g in range(8):
            kb.pend.append(lambda c=c, nq=nq, g=g: scan_unit(c, nq, g))

    def scan_unit(c, nq, g):
        S, W = lag(c)
        mcc, mbb = rst["mcc"], rst["mbb"]
        if True:
            if nq == 0:
                mdt = kb.emit("vector",
                              lambda e, g=g, W=W: e.tensor_tensor(
                                  m_dtu[:, g * STR:g * STR + W],
                                  m_delta[:, g * STR:g * STR + W],
                                  m_xsconv[:, g * STR:g * STR + W], ALU.mult),
                              waits=[rst["delta_mark"]], inc=sV)
                rst["dtu_mark"] = mdt
            par = (nq * 8 + g) % 2
            base = par * NQ * STR
            mq = None
            for nl in range(NQ):
                mq = kb.emit("vector",
                             lambda e, g=g, nl=nl, W=W: e.tensor_tensor(
                                 m_dBu[:, nl * STR:nl * STR + W],
                                 m_dtu[:, g * STR:g * STR + W],
                                 m_brep[:, nl * STR:nl * STR + W], ALU.mult),
                             waits=[mcc, mbb, rst["exp_marks"][(nq, g)],
                                    rst.get("dtu_mark")]
                             if nl == 0 else [],
                             inc=(sV if nl == NQ - 1 else None))
            for nl in range(NQ):
                n = nq * NQ + nl
                mq = kb.emit("vector",
                             lambda e, b=base, nl=nl, g=g, n=n, W=W:
                             e.tensor_tensor_scan(
                                 m_h[:, nl * STR:nl * STR + W],
                                 m_dA[:, b + nl * STR:b + nl * STR + W],
                                 m_dBu[:, nl * STR:nl * STR + W],
                                 m_hst[:, g * 16 + n:g * 16 + n + 1],
                                 ALU.mult, ALU.add),
                             waits=[mq] if nl == 0 else [],
                             inc=(sV if nl == NQ - 1 else None))
            kb.emit("vector",
                    lambda e, g=g, nq=nq, W=W: e.tensor_copy(
                        m_hst[:, g * 16 + nq * NQ:g * 16 + nq * NQ + NQ],
                        ap(m_h, W - 1, [[NQ * STR, 128], [STR, NQ]])),
                    waits=[mq])
            for nl in range(NQ):
                mq = kb.emit("vector",
                             lambda e, b=base, nl=nl, W=W: e.tensor_tensor(
                                 m_dA[:, b + nl * STR:b + nl * STR + W],
                                 m_h[:, nl * STR:nl * STR + W],
                                 m_crep[:, nl * STR:nl * STR + W], ALU.mult),
                             inc=(sV if nl == NQ - 1 else None))
            mq = kb.emit("vector",
                         lambda e, b=base, W=W: e.tensor_reduce(
                             m_ytmp[:, 0:W],
                             ap(m_dA, b,
                                [[2 * NQ * STR, 128], [1, W], [STR, NQ]]),
                             AX.X, ALU.add),
                         waits=[mq], inc=sV)
            if nq == 0:
                mv = kb.emit("vector",
                             lambda e, g=g, W=W: e.tensor_copy(
                                 m_y[:, g * STR:g * STR + W], m_ytmp[:, 0:W]),
                             waits=[mq] + ([rst["y_dump"]] if g == 0 else []),
                             inc=sV)
            else:
                mv = kb.emit("vector",
                             lambda e, g=g, W=W: e.tensor_add(
                                 m_y[:, g * STR:g * STR + W],
                                 m_y[:, g * STR:g * STR + W], m_ytmp[:, 0:W]),
                             waits=[mq], inc=sV)
            rst["dAslot_free"][par] = mv
        rst["brep_free"] = (sV, sV.v)
        if nq == 0:
            rst["dtu_done"] = (sV, sV.v)

    def scan_finish(c):
        S, W = lag(c)
        mv = None
        for g in range(8):
            mv = kb.emit("vector",
                         lambda e, g=g, W=W: e.scalar_tensor_tensor(
                             m_ybf[:, g * STR:g * STR + W],
                             m_xsconv[:, g * STR:g * STR + W],
                             c_d[:, g:g + 1], m_y[:, g * STR:g * STR + W],
                             ALU.mult, ALU.add),
                         waits=[rst["attn_mm_done"]] if g == 0 else [],
                         inc=(sV if g == 7 else None))
        rst["ybf"][c] = mv
        rst["ybf_all"] = mv
        if debug:
            rst["y_dump"] = dma(
                "dbgs",
                ap(dbg["d_y"].ap(), c * 128 * 8 * STR,
                   [[8 * STR, 128], [1, 8 * STR]]),
                m_y[:], waits=[mv])
            dma("dbgs",
                ap(dbg["d_delta"].ap(), c * 128 * 8 * STR,
                   [[8 * STR, 128], [1, 8 * STR]]),
                m_delta[:], waits=[mv])

    def mamba_attn(c):
        S, W = lag(c)
        ev_marks = {}
        grp = None
        for m in range(8):
            g_i = rst["gwout"]
            rst["gwout"] += 2
            s_lo, s_hi = g_i % 2, (g_i + 1) % 2
            mklo = dma(f"wo{s_lo}", w_rout[:, s_lo * 1024:(s_lo + 1) * 1024],
                       ap(cb_wout, m * 128 * 16 * 128,
                          [[16 * 128, 128], [1, 8 * 128]]),
                       waits=[castmark["wout"], rst["woutslot"].get(g_i - 2)])
            mkhi = dma(
                f"wo{s_hi}", w_rout[:, s_hi * 1024:(s_hi + 1) * 1024],
                ap(cb_wout, m * 128 * 16 * 128 + 8 * 128,
                   [[16 * 128, 128], [1, 8 * 128]]),
                waits=[rst["woutslot"].get(g_i + 1 - 2)])
            bank = (4, 5, 6)[m % 3]
            lo_done = None
            for k in range(16):
                rh = m_ybf if k < 8 else m_zbf
                ko = (k % 8) * STR
                s = s_lo if k < 8 else s_hi
                kk = k % 8
                grp = kb.emit(
                    "tensor",
                    lambda e, b=bank, s=s, kk=kk, rh=rh, ko=ko, W=W,
                    k=k: e.matmul(
                        M[b][:, 0:W],
                        w_rout[:, s * 1024 + kk * 128:s * 1024 + kk * 128 +
                               128],
                        rh[:, ko:ko + W], start=(k == 0), stop=(k == 15)),
                    waits=([mklo, mkhi,
                            rst["ybf"][c]] + bwait(bank))
                    if k == 0 else [],
                    inc=(sT if k in (7, 15) else None))
                if k == 7:
                    lo_done = grp
            rst["woutslot"][g_i] = lo_done
            rst["woutslot"][g_i + 1] = grp
            mv = kb.emit("vector",
                         lambda e, b=bank, m=m, W=W: e.tensor_scalar(
                             m_ast[:, m * STR:m * STR + W], M[b][:, 0:W],
                             c_bout[:, m:m + 1], None, ALU.add),
                         waits=[grp, rst["attn_tr_last"]], inc=sV)
            bankfree[bank] = mv
            ev_marks[m] = mv
        rst["attn_mm_done"] = grp
        nts = (W + 127) // 128
        for ts in range(nts):
            tw = min(128, W - ts * 128)
            mxl = dma("xlg",
                      ap(m_xlag, 0, [[DM, tw], [1, DM]]),
                      ap(t_x.ap(), (S + ts * 128) * DM, [[DM, tw], [1, DM]]),
                      waits=[rst["xlag_free"]])
            for mg in range(2):
                bank = 6 + mg
                grp = None
                for q in range(4):
                    m = mg * 4 + q
                    grp = kb.emit(
                        "tensor",
                        lambda e, b=bank, q=q, m=m, ts=ts, tw=tw: e.transpose(
                            M[b][0:tw, q * 128:q * 128 + 128],
                            m_ast[:, m * STR + ts * 128:
                                  m * STR + ts * 128 + tw], c_idn[:]),
                        waits=([ev_marks[mg * 4 + 3]] + bwait(bank))
                        if q == 0 else [],
                        inc=(sT if q == 3 else None))
                rst["attn_tr_last"] = grp
                mv = kb.emit(
                    "vector",
                    lambda e, b=bank, mg=mg, tw=tw: e.tensor_copy(
                        ap(m_ptm, mg * 512, [[DM, tw], [1, 512]]),
                        ap(M[b], 0, [[512, tw], [1, 512]])),
                    waits=[grp, rst["ptm_free"]], inc=sV)
                bankfree[bank] = mv
            mv = kb.emit("gpsimd",
                         lambda e, tw=tw: e.tensor_add(
                             ap(m_ptm, 0, [[DM, tw], [1, DM]]),
                             ap(m_ptm, 0, [[DM, tw], [1, DM]]),
                             ap(m_xlag, 0, [[DM, tw], [1, DM]])),
                         waits=[mxl, mv], inc=sGp)
            rst["xlag_free"] = mv
            mp = dma("ptw",
                     ap(pdram.ap(), (S + ts * 128) * DM, [[DM, tw], [1, DM]]),
                     ap(m_ptm, 0, [[DM, tw], [1, DM]]),
                     waits=[mv])
            rst["partial"][(c, ts)] = mp
            rst["ptm_free"] = mp

    def merge(c):
        S, W = lag(c)
        nts = (W + 127) // 128
        for ts in range(nts):
            tw = min(128, W - ts * 128)
            r0 = S + ts * 128
            f0, f1 = r0 // TCF, (r0 + tw - 1) // TCF
            mp1 = dma("mpt",
                      ap(m_ptm, 0, [[DM, tw], [1, DM]]),
                      ap(pdram.ap(), r0 * DM, [[DM, tw], [1, DM]]),
                      waits=[rst["ptm_free"], rst["partial"][(c, ts)]])
            rst["ptm_free"] = mp1
            mp2 = dma("xlg",
                      ap(m_xlag, 0, [[DM, tw], [1, DM]]),
                      ap(fdram.ap(), r0 * DM, [[DM, tw], [1, DM]]),
                      waits=[rst["xlag_free"]] + rst["fwrit"][f0] +
                      rst["fwrit"][f1])
            mv = kb.emit("gpsimd",
                         lambda e, tw=tw: e.tensor_add(
                             ap(m_ptm, 0, [[DM, tw], [1, DM]]),
                             ap(m_ptm, 0, [[DM, tw], [1, DM]]),
                             ap(m_xlag, 0, [[DM, tw], [1, DM]])),
                         waits=[mp1, mp2], inc=sGp)
            rst["xlag_free"] = mv
            mw = dma("outw",
                     ap(t_out.ap(), r0 * DM, [[DM, tw], [1, DM]]),
                     ap(m_ptm, 0, [[DM, tw], [1, DM]]),
                     waits=[mv])
            rst["ptm_free"] = mw

    def drain(n=1):
        while n > 0 and kb.pend:
            kb.pend.pop(0)()
            n -= 1

    def flush():
        drain(10 ** 9)

    def ffn_expert(f, e):
        sh = (e == 8)
        w1t, w3t, w2t = ((cb_w1s, cb_w3s, cb_w2s) if sh else
                         (cb_w1, cb_w3, cb_w2))
        eoff1 = 0 if sh else e * 32 * 128 * 8 * 128
        eoff2 = 0 if sh else e * 8 * 128 * 32 * 128
        cmk = castmark[("e", e)]
        W1S = ("w1a", "w1b", "w1c", "w1d")
        hid_last = None
        for m in range(32):
            gi1 = rst["gw13"]
            rst["gw13"] += 2
            s1, s3 = gi1 % 4, (gi1 + 1) % 4
            mk1 = dma(W1S[s1], w_r1[:, s1 * 1024:(s1 + 1) * 1024],
                      ap(w1t, eoff1 + m * 128 * 8 * 128,
                         [[8 * 128, 128], [1, 8 * 128]]),
                      waits=[cmk, rst["w1slot"].get(gi1 - 4)])
            mk3 = dma(W1S[s3], w_r1[:, s3 * 1024:(s3 + 1) * 1024],
                      ap(w3t, eoff1 + m * 128 * 8 * 128,
                         [[8 * 128, 128], [1, 8 * 128]]),
                      waits=[rst["w1slot"].get(gi1 + 1 - 4)])
            if debug and f == 0 and e == 0 and m == 0:
                rst["wr1_dump"] = dma("dbgs", dbg["d_wr1"].ap(),
                                      w_r1[:, s1 * 1024:(s1 + 1) * 1024],
                                      waits=[mk1])
            w13m = (mk1, mk3)
            b1, b3 = (0, 1, 4)[m % 3], (2, 3, 5)[m % 3]
            gslot = m % 3
            grp1 = None
            for k in range(8):
                grp1 = kb.emit(
                    "tensor",
                    lambda e_, b=b1, s=s1, k=k: e_.matmul(
                        M[b][:, 0:TCF],
                        w_r1[:, s * 1024 + k * 128:s * 1024 + k * 128 + 128],
                        f_xn[:, k * TCF:(k + 1) * TCF],
                        start=(k == 0), stop=(k == 7)),
                    waits=([w13m[0], w13m[1], rst["xn_ready"],
                            rst.get("wr1_dump")] + bwait(b1))
                    if k == 0 else [],
                    inc=(sT if k == 7 else None))
            grp = None
            for k in range(8):
                grp = kb.emit(
                    "tensor",
                    lambda e_, b=b3, s=s3, k=k: e_.matmul(
                        M[b][:, 0:TCF],
                        w_r1[:, s * 1024 + k * 128:s * 1024 + k * 128 + 128],
                        f_xn[:, k * TCF:(k + 1) * TCF],
                        start=(k == 0), stop=(k == 7)),
                    waits=bwait(b3) if k == 0 else [],
                    inc=(sT if k == 7 else None))
            rst["w1slot"][gi1] = grp
            rst["w1slot"][gi1 + 1] = grp
            if m == 31:
                rst["xn_users"].append(grp)
            ms = kb.emit("scalar",
                         lambda e_, b=b1, gs=gslot: e_.activation(
                             f_gsb[:, gs * TCF:(gs + 1) * TCF],
                             M[b][:, 0:TCF], AF.Silu),
                         waits=[grp1, rst["gsb_free"].get(gslot)],
                         inc=sS)
            if debug and f == 0 and e == 0 and m == 0:
                dma("dbgs", dbg["d_gsb"].ap(), f_gsb[:, 0:TCF], waits=[ms])
                dma("dbgs", dbg["d_xn2"].ap(), f_xn[:], waits=[ms])
            mv = kb.emit("vector",
                         lambda e_, b=b3, gs=gslot, m=m: e_.tensor_tensor(
                             f_hid[:, m * TCF:(m + 1) * TCF],
                             f_gsb[:, gs * TCF:(gs + 1) * TCF],
                             M[b][:, 0:TCF], ALU.mult),
                         waits=[ms, grp, rst.get("hid_dump")], inc=sV)
            bankfree[b1] = mv
            bankfree[b3] = mv
            rst["gsb_free"][gslot] = mv
            hid_last = mv
            drain(2)
        if debug and f == 0 and e == 0:
            rst["hid_dump"] = dma("dbgs", dbg["d_hid"].ap(), f_hid[:],
                                  waits=[hid_last])
        gb_slot = e % 2
        gbm = rst["gb_mark"].get(e)
        for half in range(2):
            grps = {}
            for m4 in range(4):
                m = half * 4 + m4
                gi2 = rst["gw2"]
                rst["gw2"] += 2
                s_lo, s_hi = gi2 % 4, (gi2 + 1) % 4
                mklo = dma(f"w2{'abcd'[s_lo]}",
                           w_r2[:, s_lo * 2048:(s_lo + 1) * 2048],
                           ap(w2t, eoff2 + m * 128 * 32 * 128,
                              [[32 * 128, 128], [1, 16 * 128]]),
                           waits=[cmk, rst["w2slot"].get(gi2 - 4)])
                mkhi = dma(f"w2{'abcd'[s_hi]}",
                           w_r2[:, s_hi * 2048:(s_hi + 1) * 2048],
                           ap(w2t, eoff2 + m * 128 * 32 * 128 + 16 * 128,
                              [[32 * 128, 128], [1, 16 * 128]]),
                           waits=[rst["w2slot"].get(gi2 + 1 - 4)])
                bank = 4 + m4
                grp = None
                lo_done = None
                for k in range(32):
                    s = s_lo if k < 16 else s_hi
                    kk = k % 16
                    grp = kb.emit(
                        "tensor",
                        lambda e_, b=bank, s=s, kk=kk, k=k: e_.matmul(
                            M[b][:, 0:TCF],
                            w_r2[:, s * 2048 + kk * 128:
                                 s * 2048 + kk * 128 + 128],
                            f_hid[:, k * TCF:(k + 1) * TCF],
                            start=(k == 0), stop=(k == 31)),
                        waits=([mklo, mkhi, hid_last] + bwait(bank))
                        if k == 0 else [],
                        inc=(sT if k in (15, 31) else None))
                    if k == 15:
                        lo_done = grp
                grps[m4] = grp
                rst["w2slot"][gi2] = lo_done
                rst["w2slot"][gi2 + 1] = grp
            for m4 in range(4):
                m = half * 4 + m4
                bank = 4 + m4
                d0 = m * TCF
                if e == 0:
                    mv = kb.emit(
                        "vector",
                        lambda e_, b=bank, d0=d0, gs=gb_slot: e_.tensor_tensor(
                            f_acc[:, d0:d0 + TCF], M[b][:, 0:TCF],
                            f_gb[:, gs * TCF:(gs + 1) * TCF], ALU.mult),
                        waits=[grps[m4], gbm, rst["acc_free"],
                               rst["acc_dump"]],
                        inc=sV)
                    bankfree[bank] = mv
                elif sh:
                    mv = kb.emit(
                        "vector",
                        lambda e_, b=bank, d0=d0: e_.scalar_tensor_tensor(
                            f_acc[:, d0:d0 + TCF], M[b][:, 0:TCF], 0.0625,
                            f_acc[:, d0:d0 + TCF], ALU.mult, ALU.add),
                        waits=[grps[m4], rst[("accev", e - 1, m)]], inc=sV)
                    bankfree[bank] = mv
                else:
                    mq = kb.emit(
                        "vector",
                        lambda e_, b=bank, gs=gb_slot: e_.tensor_tensor(
                            f_wt, M[b][:, 0:TCF],
                            f_gb[:, gs * TCF:(gs + 1) * TCF], ALU.mult),
                        waits=[grps[m4], gbm, rst["attn_tr_last"],
                               rst.get("fwt_free")], inc=sV)
                    mv = kb.emit(
                        "gpsimd",
                        lambda e_, d0=d0: e_.tensor_add(
                            f_acc[:, d0:d0 + TCF], f_acc[:, d0:d0 + TCF],
                            f_wt),
                        waits=[mq, rst[("accev", e - 1, m)]], inc=sGp)
                    bankfree[bank] = mq
                    rst["fwt_free"] = mv
                rst[("accev", e, m)] = mv
                drain(2)
        if e < 7:
            nslot = (e + 1) % 2
            rst["gb_mark"][e + 1] = dma(
                f"gb{nslot}",
                ap(f_gb, nslot * TCF, [[2 * TCF, 128], [1, TCF]]),
                ap(gdram.ap(), (e + 1) * TSEQ + f * TCF, [[0, 128], [1, TCF]]),
                waits=[gdram_mark, rst.get(("gbfree", nslot))])
        if not sh:
            rst[("gbfree", gb_slot)] = rst[("accev", e, 7)]

    def ffn_finale(f):
        last_t = None
        fw = []
        for tt in range(4):
            slot = tt % 2
            mvs = []
            for mg in range(2):
                bank = 6 + mg
                grp = None
                for q in range(4):
                    m = mg * 4 + q
                    grp = kb.emit(
                        "tensor",
                        lambda e_, b=bank, q=q, m=m, tt=tt: e_.transpose(
                            M[b][:, q * 128:q * 128 + 128],
                            f_acc[:, m * TCF + tt * 128:
                                  m * TCF + tt * 128 + 128], c_idn[:]),
                        waits=([rst[("accev", 8, 7)]] + bwait(bank))
                        if q == 0 else [],
                        inc=(sT if q == 3 else None))
                last_t = grp
                mv = kb.emit(
                    "vector",
                    lambda e_, b=bank, mg=mg: e_.tensor_copy(
                        ap(f_tm, mg * 512, [[DM, 128], [1, 512]]),
                        ap(M[b], 0, [[512, 128], [1, 512]])),
                    waits=[grp, rst.get(("ftmfree", 0))],
                    inc=sV)
                bankfree[bank] = mv
                mvs.append(mv)
            mw = dma(f"ftw{slot}",
                     ap(fdram.ap(), (f * TCF + tt * 128) * DM,
                        [[DM, 128], [1, DM]]),
                     ap(f_tm, 0, [[DM, 128], [1, DM]]),
                     waits=[mvs[-1]])
            rst[("ftmfree", 0)] = mw
            fw.append(mw)
        rst["fwrit"][f] = fw
        rst["acc_free"] = last_t
        if debug:
            rst["acc_dump"] = dma(
                "dbgs",
                ap(dbg["d_acc"].ap(), f * 128 * 8 * TCF,
                   [[8 * TCF, 128], [1, 8 * TCF]]),
                f_acc[:], waits=[(sV, sV.v)])

    for r in range(max_rounds):
        c = r
        f = r // 2
        ehalf = r % 2
        if 0 <= c - 1 < NCM and (c - 1) in rst["ybf"] and not no_mamba:
            mamba_attn(c - 1)
        if 0 <= c - 2 < NCM and all(
                ff in rst["fwrit"] for ff in
                ((lag(c - 2)[0]) // TCF,
                 (lag(c - 2)[0] + lag(c - 2)[1] - 1) // TCF)):
            merge(c - 2)
        if r >= NCM:
            continue
        if ehalf == 0:
            xn_build(f)
            rst["gb_mark"] = {}
            rst["gb_mark"][0] = dma(
                "gb0", ap(f_gb, 0, [[2 * TCF, 128], [1, TCF]]),
                ap(gdram.ap(), f * TCF, [[0, 128], [1, TCF]]),
                waits=[gdram_mark, rst.get(("gbfree", 0))])
        if not no_mamba:
            mamba_xz_conv(c)
        elist = list(range(0, 5)) if ehalf == 0 else list(range(5, 9))
        for idx, e in enumerate(elist):
            ffn_expert(f, e)
            if no_mamba:
                continue
            if idx == 0:
                mamba_xdbl(c)
            elif idx == 1:
                mamba_delta(c)
            elif idx == 2:
                scan_nq(c, 0)
                scan_nq(c, 1)
            elif idx == 3:
                scan_nq(c, 2)
                scan_nq(c, 3)
                kb.pend.append(lambda c=c: scan_finish(c))
        flush()
        if ehalf == 1:
            ffn_finale(f)

    outc = SQ["outw"][1]
    kb.emit("gpsimd", lambda e: e.engine_nop(), waits=[(outc, outc.v)])

    with nc.Block() as block:
        @block.sync
        def _(eo):
            for fn_ in kb.prog["sync"]:
                fn_(eo)

        @block.gpsimd
        def _(eo):
            for fn_ in kb.prog["gpsimd"]:
                fn_(eo)

        @block.vector
        def _(eo):
            for fn_ in kb.prog["vector"]:
                fn_(eo)

        @block.scalar
        def _(eo):
            for fn_ in kb.prog["scalar"]:
                fn_(eo)

        @block.tensor
        def _(eo):
            for fn_ in kb.prog["tensor"]:
                fn_(eo)

    es.close()
    return nc, kb


def prep_inputs(inp):
    f32 = np.float32
    W_xp = np.asarray(inp["L0_attn_W_xp"], f32)
    wxp = np.zeros((DM, 128), f32)
    wxp[:, 0:64] = W_xp[:, 0:64]
    wxp[:, 64:80] = W_xp[:, 64:80]
    wxp[:, 96:112] = W_xp[:, 80:96]
    A_log = np.asarray(inp["L0_attn_A_log"], f32)
    aprep = np.ascontiguousarray(
        A_log.reshape(8, 128, 16).transpose(1, 0, 2).reshape(128, 128))

    def col8(v):
        return np.ascontiguousarray(np.asarray(v, f32).reshape(8, 128).T)

    def kcol(k):
        return np.ascontiguousarray(
            np.asarray(k, f32).reshape(4, 8, 128).transpose(2, 1, 0)
            .reshape(128, 32))

    shared = {
        "w_in": np.ascontiguousarray(np.asarray(inp["L0_attn_W_in"], f32)),
        "w_out": np.ascontiguousarray(np.asarray(inp["L0_attn_W_out"], f32)),
        "w1e": np.ascontiguousarray(np.asarray(inp["L0_ffn_w1_e"], f32)),
        "w3e": np.ascontiguousarray(np.asarray(inp["L0_ffn_w3_e"], f32)),
        "w2e": np.ascontiguousarray(np.asarray(inp["L0_ffn_w2_e"], f32)),
        "w1s": np.ascontiguousarray(
            np.asarray(inp["L0_ffn_w1_s"], f32).transpose(1, 0, 2)
            .reshape(DM, FF)),
        "w3s": np.ascontiguousarray(
            np.asarray(inp["L0_ffn_w3_s"], f32).transpose(1, 0, 2)
            .reshape(DM, FF)),
        "w2s": np.ascontiguousarray(
            np.asarray(inp["L0_ffn_w2_s"], f32).reshape(FF, DM)),
        "wxp": wxp,
        "wdt": np.ascontiguousarray(np.asarray(inp["L0_attn_W_dt"], f32)),
        "gwt": np.ascontiguousarray(np.asarray(inp["L0_ffn_gate_W"], f32).T),
        "aprep": aprep,
        "dcol": col8(inp["L0_attn_D_param"]),
        "bdtcol": col8(inp["L0_attn_b_dt"]),
        "boutcol": col8(inp["L0_attn_b_out"]),
        "kxcol": kcol(inp["L0_attn_K_x"]),
        "kzcol": kcol(inp["L0_attn_K_z"]),
        "idn": np.eye(128, dtype=f32),
        "cvec": np.broadcast_to(
            np.array([1e-5, 1.0], f32), (128, 2)).copy(),
    }
    x = np.asarray(inp["L0_attn_x"], f32)
    maps = []
    for b in range(8):
        m = dict(shared)
        m["x"] = np.ascontiguousarray(x[b])
        maps.append(m)
    return maps


def kernel(**inputs):
    maps = prep_inputs(inputs)
    nc, kb = build_core(debug=False)
    res = run_bass_kernel_spmd(nc, maps, core_ids=list(range(8)))
    out = np.stack([np.asarray(r["out"], np.float32) for r in res.results])
    return out
